# version 01
# baseline (speedup 1.0000x reference)
import numpy as np
from contextlib import ExitStack
from collections import defaultdict

import concourse.bass as bass
import concourse.mybir as mybir
from concourse.bass_utils import run_bass_kernel_spmd
import ml_dtypes

F32 = mybir.dt.float32
BF16 = mybir.dt.bfloat16
I32 = mybir.dt.int32
I16 = mybir.dt.int16
AF = mybir.ActivationFunctionType
ALU = mybir.AluOpType
bfloat16 = ml_dtypes.bfloat16

ENGINES = ("sync", "vector", "scalar", "tensor", "gpsimd")


class Sched:

    def __init__(self):
        self.q = {e: [] for e in ENGINES}
        self.cnt = defaultdict(int)

    def op(self, eng, fn, wait=(), inc=None, amt=1):
        self.q[eng].append((list(wait), fn, inc, amt))
        if inc is not None:
            self.cnt[inc] += amt

    def dma(self, eng, out, in_, sem, wait=()):
        self.op(eng, (lambda out, in_: lambda q: q.dma_start(out=out, in_=in_))(out, in_),
                wait=wait, inc=sem, amt=16)

    def c(self, sem):
        return self.cnt[sem]

    def emit_all(self, block, sems):
        sched = self

        def mk(name):
            def emit(q):
                for waits, fn, inc, amt in sched.q[name]:
                    for s, v in waits:
                        if v > 0:
                            q.wait_ge(sems[s], v)
                    ins = fn(q)
                    if inc is not None:
                        assert ins is not None, f"op on {name} must return last instruction"
                        ins.then_inc(sems[inc], amt)
            return emit

        for name in ENGINES:
            if self.q[name]:
                getattr(block, name)(mk(name))



def build_wmat(n_gu_cols=688, n_d_cols=256):
    nc = bass.Bass("TRN2", target_bir_lowering=False, detect_race_conditions=False)
    ins = {}
    outs = {}
    specs = []
    for L in (0, 1):
        for nm, rows, cols in (("g", 2048, n_gu_cols), ("u", 2048, n_gu_cols), ("d", 5504, n_d_cols)):
            key = f"L{L}{nm}"
            ins[key] = nc.dram_tensor(f"idx_{key}", [rows, cols], I32, kind="ExternalInput").ap()
            outs[key] = nc.dram_tensor(f"w_{key}", [rows, cols], BF16, kind="ExternalOutput").ap()
            specs.append((key, rows, cols, L))
    lut0 = nc.dram_tensor("lut0", [128, 16], F32, kind="ExternalInput").ap()
    lut1 = nc.dram_tensor("lut1", [128, 16], F32, kind="ExternalInput").ap()

    CMAX = max(n_gu_cols, n_d_cols)
    with ExitStack() as es:
        idxt = [es.enter_context(nc.sbuf_tensor(f"idxt{i}", [128, CMAX], I32)) for i in range(2)]
        idxh = [es.enter_context(nc.sbuf_tensor(f"idxh{i}", [128, CMAX], I16)) for i in range(2)]
        acc = [es.enter_context(nc.sbuf_tensor(f"acc{i}", [128, CMAX], BF16)) for i in range(2)]
        tj = es.enter_context(nc.sbuf_tensor("tj", [128, CMAX], BF16))
        luts = [es.enter_context(nc.sbuf_tensor(f"lutsb{i}", [128, 16], F32)) for i in range(2)]
        sems = {n: es.enter_context(nc.semaphore(n)) for n in ("LA", "ST", "V")}
        block = es.enter_context(nc.Block())

        sc = Sched()
        sc.dma("sync", luts[0][:], lut0[:, :], "LA")
        sc.dma("sync", luts[1][:], lut1[:, :], "LA")

        chunks = []
        for key, rows, cols, L in specs:
            for r0 in range(0, rows, 128):
                chunks.append((key, r0, min(128, rows - r0), cols, L))

        v_done = [0, 0]
        s_done = [0, 0]
        for i, (key, r0, pr, cols, L) in enumerate(chunks):
            b = i % 2
            sc.dma("sync", idxt[b][:pr, :cols], ins[key][r0:r0 + pr, :], "LA",
                   wait=[("V", v_done[b])])
            la_here = sc.c("LA")

            def gather(q, b=b, pr=pr, cols=cols, L=L):
                q.tensor_copy(out=idxh[b][:pr, :cols], in_=idxt[b][:pr, :cols])
                lt = luts[L]
                q.tensor_scalar(out=acc[b][:pr, :cols], in0=idxh[b][:pr, :cols],
                                scalar1=0.0, scalar2=lt[:pr, 0:1], op0=ALU.is_equal, op1=ALU.mult)
                last = None
                for j in range(1, 16):
                    q.tensor_scalar(out=tj[:pr, :cols], in0=idxh[b][:pr, :cols],
                                    scalar1=float(j), scalar2=lt[:pr, j:j + 1],
                                    op0=ALU.is_equal, op1=ALU.mult)
                    last = q.tensor_tensor(out=acc[b][:pr, :cols], in0=acc[b][:pr, :cols],
                                           in1=tj[:pr, :cols], op=ALU.add)
                return last
            sc.op("vector", gather, wait=[("LA", la_here), ("ST", s_done[b])], inc="V")
            v_done[b] = sc.c("V")
            sc.dma("sync", outs[key][r0:r0 + pr, :], acc[b][:pr, :cols], "ST",
                   wait=[("V", v_done[b])])
            s_done[b] = sc.c("ST")

        sc.op("vector", lambda q: q.engine_nop(), wait=[("ST", sc.c("ST"))])
        sc.emit_all(block, sems)
    return nc



def build_ffn(NT=1024, wo_prologue=False):
    IT = 44
    NH = NT // 512
    nc = bass.Bass("TRN2", target_bir_lowering=False, detect_race_conditions=False)
    xin = nc.dram_tensor("xin", [16, 128, NT], F32, kind="ExternalInput").ap()
    wg = nc.dram_tensor("wg", [IT // 2, 16, 128, 256], BF16, kind="ExternalInput").ap()
    wu = nc.dram_tensor("wu", [IT // 2, 16, 128, 256], BF16, kind="ExternalInput").ap()
    wd = nc.dram_tensor("wd", [IT, 128, 2048], BF16, kind="ExternalInput").ap()
    y = nc.dram_tensor("y", [16, 128, NT], F32, kind="ExternalOutput").ap()
    if wo_prologue:
        o_fm = nc.dram_tensor("o_fm", [12, 128, NT], BF16, kind="ExternalInput").ap()
        woT = nc.dram_tensor("woT", [12, 128, 2048], F32, kind="ExternalInput").ap()
        xmid = nc.dram_tensor("xmid", [16, 128, NT], F32, kind="Internal").ap()

    with ExitStack() as es:
        xt = [es.enter_context(nc.sbuf_tensor(f"xt{i}", [128, NT], F32)) for i in range(2)]
        sq = es.enter_context(nc.sbuf_tensor("sq", [128, NT], F32))
        wgt = [es.enter_context(nc.sbuf_tensor(f"wgt{i}", [128, 16, 256], BF16)) for i in range(2)]
        wut = [es.enter_context(nc.sbuf_tensor(f"wut{i}", [128, 16, 256], BF16)) for i in range(2)]
        wdt = [es.enter_context(nc.sbuf_tensor(f"wdt{i}", [128, 2048], BF16)) for i in range(2)]
        sil = es.enter_context(nc.sbuf_tensor("sil", [128, NT], F32))
        ysb = [es.enter_context(nc.sbuf_tensor(f"ysb{i}", [128, 512], F32)) for i in range(2)]
        ones = es.enter_context(nc.sbuf_tensor("ones", [128, 1], F32))
        onesr = es.enter_context(nc.sbuf_tensor("onesr", [1, 128], F32))
        epsc = es.enter_context(nc.sbuf_tensor("epsc", [128, 1], F32))
        mrow = es.enter_context(nc.sbuf_tensor("mrow", [1, NT], F32))
        rrow = es.enter_context(nc.sbuf_tensor("rrow", [1, NT], F32))
        msq = es.enter_context(nc.sbuf_tensor("msq", [1, NT], F32))
        mbc = es.enter_context(nc.sbuf_tensor("mbc", [128, NT], F32))
        rbc = es.enter_context(nc.sbuf_tensor("rbc", [128, NT], F32))
        ps_es = ExitStack()
        ps_a = ps_es.enter_context(nc.psum_tensor("ps_a", [128, NT], F32))
        ps_b = ps_es.enter_context(nc.psum_tensor("ps_b", [128, NT], F32))
        sems = {n: es.enter_context(nc.semaphore(n)) for n in
                ("LA", "LW", "ST", "V", "S", "T", "G")}
        sc = Sched()

        sc.op("gpsimd", lambda q: q.memset(ones[:], 1.0), inc="G")
        sc.op("gpsimd", lambda q: q.memset(onesr[:], 1.0), inc="G")
        sc.op("gpsimd", lambda q: q.memset(epsc[:], 1e-5), inc="G")
        G0 = sc.c("G")

        xsrc = xmid if wo_prologue else xin

        pro_es = ExitStack()
        if wo_prologue:
            wot = pro_es.enter_context(nc.sbuf_tensor("wot", [128, 12, 2048], BF16))
            osb = pro_es.enter_context(nc.sbuf_tensor("osb", [128, 12, NT], BF16))
            for k in range(12):
                sc.dma("gpsimd", wot[:, k, :], woT[k], "LW")
                sc.dma("sync", osb[:, k, :], o_fm[k], "LA")
            lw0, la0 = sc.c("LW"), sc.c("LA")
            evac_v = 0
            for m in range(16):
                sc.dma("sync", xt[m % 2][:, :], xin[m], "LA", wait=[("V", evac_v)])
                la_x = sc.c("LA")

                def mmwo(q, m=m):
                    ins = None
                    for n2 in range(NH):
                        for k in range(12):
                            ins = q.matmul(out=ps_a[:, n2 * 512:(n2 + 1) * 512],
                                           lhsT=wot[:, k, m * 128:(m + 1) * 128],
                                           rhs=osb[:, k, n2 * 512:(n2 + 1) * 512],
                                           start=(k == 0), stop=(k == 11))
                    return ins
                sc.op("tensor", mmwo, wait=[("LW", lw0), ("LA", la0), ("V", evac_v)], inc="T")
                t_here = sc.c("T")

                def evac(q, m=m):
                    return q.tensor_tensor(out=xt[m % 2][:, :], in0=ps_a[:, :],
                                           in1=xt[m % 2][:, :], op=ALU.add)
                sc.op("vector", evac, wait=[("T", t_here), ("LA", la_x)], inc="V")
                evac_v = sc.c("V")
                sc.dma("sync", xmid[m], xt[m % 2][:, :], "ST", wait=[("V", evac_v)])
        st_pro = sc.c("ST")

        for c in range(16):
            w = [("ST", st_pro)] if c < 2 else [("T", sc.c("T"))]
            if c < 2 and wo_prologue:
                w.append(("V", sc.c("V")))
            sc.dma("sync", xt[c % 2][:, :], xsrc[c], "LA", wait=w)
            la_here = sc.c("LA")

            def sqr(q, c=c):
                return q.activation(sq[:, :], xt[c % 2][:, :], AF.Square)
            sc.op("scalar", sqr, wait=[("LA", la_here), ("T", sc.c("T"))], inc="S")
            s_here = sc.c("S")

            def mmstat(q, c=c):
                ins = None
                for n2 in range(NH):
                    q.matmul(out=ps_a[:1, n2 * 512:(n2 + 1) * 512], lhsT=ones[:, :],
                             rhs=xt[c % 2][:, n2 * 512:(n2 + 1) * 512],
                             start=(c == 0), stop=(c == 15))
                for n2 in range(NH):
                    ins = q.matmul(out=ps_b[:1, n2 * 512:(n2 + 1) * 512], lhsT=ones[:, :],
                                   rhs=sq[:, n2 * 512:(n2 + 1) * 512],
                                   start=(c == 0), stop=(c == 15))
                return ins
            mmw = [("S", s_here), ("G", G0)]
            if c == 0 and wo_prologue:
                mmw.append(("V", sc.c("V")))
            sc.op("tensor", mmstat, wait=mmw, inc="T")
        t_stats = sc.c("T")

        def rows(q):
            q.activation(mrow[:, :], ps_a[:1, :], AF.Copy, scale=1.0 / 2048.0)
            q.activation(rrow[:, :], ps_b[:1, :], AF.Copy, scale=1.0 / 2048.0)
            return q.activation(msq[:, :], mrow[:, :], AF.Square)
        sc.op("scalar", rows, wait=[("T", t_stats)], inc="S")

        def rows2(q):
            return q.tensor_tensor(out=rrow[:, :], in0=rrow[:, :], in1=msq[:, :], op=ALU.subtract)
        sc.op("vector", rows2, wait=[("S", sc.c("S"))], inc="V")

        def rows3(q):
            q.activation(rrow[:, :], rrow[:, :], AF.Ln, bias=epsc[0:1, 0:1])
            return q.activation(rrow[:, :], rrow[:, :], AF.Exp, scale=-0.5)
        sc.op("scalar", rows3, wait=[("V", sc.c("V")), ("G", G0)], inc="S")
        s_rows = sc.c("S")

        def bcast(q):
            ins = None
            for n2 in range(NH):
                q.matmul(out=ps_a[:, n2 * 512:(n2 + 1) * 512], lhsT=onesr[:, :],
                         rhs=mrow[0:1, n2 * 512:(n2 + 1) * 512], start=True, stop=True)
            for n2 in range(NH):
                ins = q.matmul(out=ps_b[:, n2 * 512:(n2 + 1) * 512], lhsT=onesr[:, :],
                               rhs=rrow[0:1, n2 * 512:(n2 + 1) * 512], start=True, stop=True)
            return ins
        sc.op("tensor", bcast, wait=[("S", s_rows)], inc="T")
        t_bc = sc.c("T")

        def cpybc(q):
            q.tensor_copy(out=mbc[:, :], in_=ps_a[:, :])
            return q.tensor_copy(out=rbc[:, :], in_=ps_b[:, :])
        sc.op("vector", cpybc, wait=[("T", t_bc)], inc="V")
        v_bc = sc.c("V")

        pro_es.close()
        xhat = es.enter_context(nc.sbuf_tensor("xhat", [128, 16, NT], BF16))
        hsb = es.enter_context(nc.sbuf_tensor("hsb", [128, IT, NT], BF16))

        for c in range(16):
            sc.dma("sync", xt[c % 2][:, :], xsrc[c], "LA",
                   wait=[("V", v_bc if c < 2 else sc.c("V"))])
            la_here = sc.c("LA")

            def apply(q, c=c):
                q.tensor_tensor(out=xt[c % 2][:, :], in0=xt[c % 2][:, :], in1=mbc[:, :],
                                op=ALU.subtract)
                return q.tensor_tensor(out=xhat[:, c, :], in0=xt[c % 2][:, :], in1=rbc[:, :],
                                       op=ALU.mult)
            sc.op("vector", apply, wait=[("LA", la_here), ("V", v_bc)], inc="V")
        v_xhat = sc.c("V")

        wpair_t = [0, 0]
        for p in range(IT // 2):
            b = p % 2
            sc.dma("sync", wgt[b][:, :, :], wg[p].rearrange("c p i -> p c i"), "LA",
                   wait=[("T", wpair_t[b])])
            sc.dma("sync", wut[b][:, :, :], wu[p].rearrange("c p i -> p c i"), "LA",
                   wait=[("T", wpair_t[b])])
            la_w = sc.c("LA")
            for half in range(2):
                it = 2 * p + half

                def mmgu(q, b=b, half=half):
                    ins = None
                    for n2 in range(NH):
                        for k in range(16):
                            q.matmul(out=ps_a[:, n2 * 512:(n2 + 1) * 512],
                                     lhsT=wgt[b][:, k, half * 128:half * 128 + 128],
                                     rhs=xhat[:, k, n2 * 512:(n2 + 1) * 512],
                                     start=(k == 0), stop=(k == 15))
                    for n2 in range(NH):
                        for k in range(16):
                            ins = q.matmul(out=ps_b[:, n2 * 512:(n2 + 1) * 512],
                                           lhsT=wut[b][:, k, half * 128:half * 128 + 128],
                                           rhs=xhat[:, k, n2 * 512:(n2 + 1) * 512],
                                           start=(k == 0), stop=(k == 15))
                    return ins
                sc.op("tensor", mmgu,
                      wait=[("LA", la_w), ("V", max(v_xhat, sc.c("V"))), ("S", sc.c("S"))],
                      inc="T")
                t_gu = sc.c("T")

                def silu_(q):
                    return q.activation(sil[:, :], ps_a[:, :], AF.Silu)
                sc.op("scalar", silu_, wait=[("T", t_gu)], inc="S")
                s_sil = sc.c("S")

                def hmul(q, it=it):
                    return q.tensor_tensor(out=hsb[:, it, :], in0=sil[:, :], in1=ps_b[:, :],
                                           op=ALU.mult)
                sc.op("vector", hmul, wait=[("S", s_sil)], inc="V")
            wpair_t[b] = sc.c("T")
        v_h_done = sc.c("V")
        s_h_done = sc.c("S")

        ps_es.close()
        ps_y = es.enter_context(nc.psum_tensor("ps_y", [128, 4096], F32))
        wd_t = [0, 0]
        ev_after = 0
        for dh in range(2):
            for th in range(NH):
                for i in range(IT):
                    b = i % 2
                    sc.dma("sync", wdt[b][:, :], wd[i], "LA", wait=[("T", wd_t[b])])
                    la_wd = sc.c("LA")

                    def mmd(q, b=b, i=i, dh=dh, th=th):
                        ins = None
                        for dt in range(8):
                            ins = q.matmul(out=ps_y[:, dt * 512:(dt + 1) * 512],
                                           lhsT=wdt[b][:, (dh * 8 + dt) * 128:(dh * 8 + dt) * 128 + 128],
                                           rhs=hsb[:, i, th * 512:th * 512 + 512],
                                           start=(i == 0), stop=(i == IT - 1))
                        return ins
                    w = [("LA", la_wd)]
                    if i == 0:
                        w += [("V", max(v_h_done, ev_after)), ("S", s_h_done)]
                    sc.op("tensor", mmd, wait=w, inc="T")
                    wd_t[b] = sc.c("T")
                t_down = sc.c("T")
                for dt in range(8):
                    m = dh * 8 + dt
                    sc.dma("sync", xt[dt % 2][:, :512], xsrc[m][:, th * 512:th * 512 + 512],
                           "LA", wait=[("V", sc.c("V"))])
                    la_x = sc.c("LA")

                    def evd(q, dt=dt):
                        return q.tensor_tensor(out=ysb[dt % 2][:, :],
                                               in0=ps_y[:, dt * 512:(dt + 1) * 512],
                                               in1=xt[dt % 2][:, :512], op=ALU.add)
                    sc.op("vector", evd, wait=[("T", t_down), ("LA", la_x), ("ST", sc.c("ST"))],
                          inc="V")
                    sc.dma("sync", y[m][:, th * 512:th * 512 + 512], ysb[dt % 2][:, :], "ST",
                           wait=[("V", sc.c("V"))])
                ev_after = sc.c("V")

        sc.op("vector", lambda q: q.engine_nop(), wait=[("ST", sc.c("ST"))])
        block = es.enter_context(nc.Block())
        sc.emit_all(block, sems)
    return nc



QKSCALE = float(1.0 / np.sqrt(128.0))


def build_mla(phases=4.0):
    NTK = 2048
    NTQ = 1024
    nc = bass.Bass("TRN2", target_bir_lowering=False, detect_race_conditions=False)
    xseq = nc.dram_tensor("xseq", [16, 128, NTK], F32, kind="ExternalInput").ap()
    wqa = nc.dram_tensor("wqa", [2048, 512], F32, kind="ExternalInput").ap()
    wqab = nc.dram_tensor("wqab", [128, 4], F32, kind="ExternalInput").ap()
    wqbn = nc.dram_tensor("wqbn", [512, 2048], F32, kind="ExternalInput").ap()
    wqbnb = nc.dram_tensor("wqbnb", [128, 16], F32, kind="ExternalInput").ap()
    wqbr = nc.dram_tensor("wqbr", [512, 1024], F32, kind="ExternalInput").ap()
    wqbrb = nc.dram_tensor("wqbrb", [1, 1024], F32, kind="ExternalInput").ap()
    wkva = nc.dram_tensor("wkva", [2048, 576], F32, kind="ExternalInput").ap()
    wkvab = nc.dram_tensor("wkvab", [128, 4], F32, kind="ExternalInput").ap()
    kpeb = nc.dram_tensor("kpeb", [1, 64], F32, kind="ExternalInput").ap()
    wkvb = nc.dram_tensor("wkvb", [512, 4096], F32, kind="ExternalInput").ap()
    wkvbb = nc.dram_tensor("wkvbb", [128, 32], F32, kind="ExternalInput").ap()
    wo = nc.dram_tensor("wo", [2048, 2048], F32, kind="ExternalInput").ap()
    wob = nc.dram_tensor("wob", [128, 16], F32, kind="ExternalInput").ap()
    cs = nc.dram_tensor("cs", [16, 128, 64], F32, kind="ExternalInput").ap()
    iden = nc.dram_tensor("iden", [128, 128], F32, kind="ExternalInput").ap()
    y = nc.dram_tensor("y", [16, 128, NTQ], F32, kind="ExternalOutput").ap()

    with ExitStack() as es:
        qa = es.enter_context(nc.sbuf_tensor("qa", [128, 4, NTQ], BF16))
        ckv = es.enter_context(nc.sbuf_tensor("ckv", [128, 4, NTK], BF16))
        qrf = es.enter_context(nc.sbuf_tensor("qrf", [64, 16, NTQ], BF16))
        krf = es.enter_context(nc.sbuf_tensor("krf", [64, NTK], BF16))
        cssb = es.enter_context(nc.sbuf_tensor("cssb", [128, 16, 64], F32))
        wqab_s = es.enter_context(nc.sbuf_tensor("wqab_s", [128, 4], F32))
        wqbnb_s = es.enter_context(nc.sbuf_tensor("wqbnb_s", [128, 16], F32))
        wqbrb_s = es.enter_context(nc.sbuf_tensor("wqbrb_s", [1, 1024], BF16))
        wkvab_s = es.enter_context(nc.sbuf_tensor("wkvab_s", [128, 4], F32))
        kpeb_s = es.enter_context(nc.sbuf_tensor("kpeb_s", [1, 64], BF16))
        wkvbb_s = es.enter_context(nc.sbuf_tensor("wkvbb_s", [128, 32], F32))
        wob_s = es.enter_context(nc.sbuf_tensor("wob_s", [128, 16], F32))
        idenb = es.enter_context(nc.sbuf_tensor("idenb", [128, 128], BF16))
        ones = es.enter_context(nc.sbuf_tensor("ones", [128, 1], F32))
        onesb = es.enter_context(nc.sbuf_tensor("onesb", [128, 1], BF16))
        onesr = es.enter_context(nc.sbuf_tensor("onesr", [1, 128], F32))
        onesrb = es.enter_context(nc.sbuf_tensor("onesrb", [1, 128], BF16))
        epsc = es.enter_context(nc.sbuf_tensor("epsc", [128, 1], F32))
        m4c = es.enter_context(nc.sbuf_tensor("m4c", [128, 1], F32))

        sems = {n: es.enter_context(nc.semaphore(n)) for n in
                ("LA", "LW", "ST", "V", "S", "T", "G")}
        sc = Sched()

        for t, v in ((ones, 1.0), (onesb, 1.0), (onesr, 1.0), (onesrb, 1.0),
                     (epsc, 1e-5), (m4c, -4.0)):
            sc.op("gpsimd", (lambda t, v: lambda q: q.memset(t[:], v))(t, v), inc="G")
        G0 = sc.c("G")
        for dst, src in ((wqab_s, wqab), (wqbnb_s, wqbnb), (wkvab_s, wkvab),
                         (wkvbb_s, wkvbb), (wob_s, wob)):
            sc.dma("sync", dst[:], src[:], "LA")
        sc.dma("sync", cssb[:], cs.rearrange("t p d -> p t d"), "LA")
        for dst, src in ((wqbrb_s, wqbrb), (kpeb_s, kpeb), (idenb, iden)):
            sc.dma("gpsimd", dst[:], src[:], "LW")
        la_small = sc.c("LA")


        def _bail():
            sc.op("vector", lambda q: q.engine_nop(), wait=[("ST", sc.c("ST"))])
            blk = es.enter_context(nc.Block())
            sc.emit_all(blk, sems)
        ab_es = ExitStack()
        xhat = ab_es.enter_context(nc.sbuf_tensor("xhat", [128, 16, NTK], BF16))
        a_es = ExitStack()
        xt = [a_es.enter_context(nc.sbuf_tensor(f"xt{i}", [128, NTK], F32)) for i in range(2)]
        sq = a_es.enter_context(nc.sbuf_tensor("sq", [128, NTK], F32))
        mbc = a_es.enter_context(nc.sbuf_tensor("mbc", [128, NTK], F32))
        rbc = a_es.enter_context(nc.sbuf_tensor("rbc", [128, NTK], F32))
        mrow = a_es.enter_context(nc.sbuf_tensor("mrow", [1, NTK], F32))
        rrow = a_es.enter_context(nc.sbuf_tensor("rrow", [1, NTK], F32))
        msq = a_es.enter_context(nc.sbuf_tensor("msq", [1, NTK], F32))
        pa_es = ExitStack()
        PA = pa_es.enter_context(nc.psum_tensor("PA", [128, NTK], F32))
        PB = pa_es.enter_context(nc.psum_tensor("PB", [128, NTK], F32))

        for c in range(16):
            sc.dma("sync", xt[c % 2][:, :], xseq[c], "LA", wait=[("T", sc.c("T"))])
            la_here = sc.c("LA")
            sc.op("scalar", (lambda c: lambda q: q.activation(sq[:, :], xt[c % 2][:, :], AF.Square))(c),
                  wait=[("LA", la_here), ("T", sc.c("T"))], inc="S")
            s_here = sc.c("S")

            def mmstat(q, c=c):
                ins = None
                for n2 in range(4):
                    q.matmul(out=PA[:1, n2 * 512:(n2 + 1) * 512], lhsT=ones[:, :],
                             rhs=xt[c % 2][:, n2 * 512:(n2 + 1) * 512],
                             start=(c == 0), stop=(c == 15))
                for n2 in range(4):
                    ins = q.matmul(out=PB[:1, n2 * 512:(n2 + 1) * 512], lhsT=ones[:, :],
                                   rhs=sq[:, n2 * 512:(n2 + 1) * 512],
                                   start=(c == 0), stop=(c == 15))
                return ins
            sc.op("tensor", mmstat, wait=[("S", s_here), ("G", G0)], inc="T")
        t_stats = sc.c("T")

        def rows(q):
            q.activation(mrow[:, :], PA[:1, :], AF.Copy, scale=1.0 / 2048.0)
            q.activation(rrow[:, :], PB[:1, :], AF.Copy, scale=1.0 / 2048.0)
            return q.activation(msq[:, :], mrow[:, :], AF.Square)
        sc.op("scalar", rows, wait=[("T", t_stats)], inc="S")
        sc.op("vector", lambda q: q.tensor_tensor(out=rrow[:, :], in0=rrow[:, :], in1=msq[:, :],
                                                  op=ALU.subtract),
              wait=[("S", sc.c("S"))], inc="V")

        def rows3(q):
            q.activation(rrow[:, :], rrow[:, :], AF.Ln, bias=epsc[0:1, 0:1])
            return q.activation(rrow[:, :], rrow[:, :], AF.Exp, scale=-0.5)
        sc.op("scalar", rows3, wait=[("V", sc.c("V")), ("G", G0)], inc="S")
        s_rows = sc.c("S")

        def bcast(q):
            ins = None
            for n2 in range(4):
                q.matmul(out=PA[:, n2 * 512:(n2 + 1) * 512], lhsT=onesr[:, :],
                         rhs=mrow[0:1, n2 * 512:(n2 + 1) * 512], start=True, stop=True)
            for n2 in range(4):
                ins = q.matmul(out=PB[:, n2 * 512:(n2 + 1) * 512], lhsT=onesr[:, :],
                               rhs=rrow[0:1, n2 * 512:(n2 + 1) * 512], start=True, stop=True)
            return ins
        sc.op("tensor", bcast, wait=[("S", s_rows)], inc="T")

        def cpybc(q):
            q.tensor_copy(out=mbc[:, :], in_=PA[:, :])
            return q.tensor_copy(out=rbc[:, :], in_=PB[:, :])
        sc.op("vector", cpybc, wait=[("T", sc.c("T"))], inc="V")
        v_bc = sc.c("V")

        for c in range(16):
            sc.dma("sync", xt[c % 2][:, :], xseq[c], "LA",
                   wait=[("V", v_bc if c < 2 else sc.c("V"))])
            la_here = sc.c("LA")

            def apply(q, c=c):
                q.tensor_tensor(out=xt[c % 2][:, :], in0=xt[c % 2][:, :], in1=mbc[:, :],
                                op=ALU.subtract)
                return q.tensor_tensor(out=xhat[:, c, :], in0=xt[c % 2][:, :], in1=rbc[:, :],
                                       op=ALU.mult)
            sc.op("vector", apply, wait=[("LA", la_here), ("V", v_bc)], inc="V")
        v_xhat = sc.c("V")
        a_es.close()
        pa_es.close()
        if phases <= 1:
            sc.op("vector", lambda q: q.engine_nop(), wait=[("ST", sc.c("ST"))])
            block = es.enter_context(nc.Block())
            sc.emit_all(block, sems)
            ab_es.close()
            return nc

        b_es = ExitStack()
        if phases == 2.021:
            PB1 = b_es.enter_context(nc.psum_tensor("PB1", [128, NTK], F32))
            _bail(); b_es.close(); ab_es.close(); return nc
        if phases == 2.024:
            tiny = b_es.enter_context(nc.sbuf_tensor("tiny", [128, 16], F32))
            _bail(); b_es.close(); ab_es.close(); return nc
        if phases == 2.022:
            wqa_s = b_es.enter_context(nc.sbuf_tensor("wqa_s", [128, 16, 512], BF16))
            _bail(); b_es.close(); ab_es.close(); return nc
        wqa_s = b_es.enter_context(nc.sbuf_tensor("wqa_s", [128, 16, 512], BF16))
        wkva_s = b_es.enter_context(nc.sbuf_tensor("wkva_s", [128, 16, 576], BF16))
        wqbr_s = b_es.enter_context(nc.sbuf_tensor("wqbr_s", [128, 4, 1024], BF16))
        qrt = b_es.enter_context(nc.sbuf_tensor("qrt", [128, 8, 1024], BF16))
        krt = b_es.enter_context(nc.sbuf_tensor("krt", [128, 16, 64], BF16))
        rt = [b_es.enter_context(nc.sbuf_tensor(f"rt{i}", [128, 64], F32)) for i in range(3)]
        pb_es = ExitStack()
        PB1 = pb_es.enter_context(nc.psum_tensor("PB1", [128, NTK], F32))
        PB2 = pb_es.enter_context(nc.psum_tensor("PB2", [128, 1024], F32))
        PTB = pb_es.enter_context(nc.psum_tensor("PTB", [64, 1024], BF16))

        if phases != 2.02:
            sc.dma("gpsimd", wqa_s[:], wqa.rearrange("(k p) d -> p k d", p=128), "LW",
                   wait=([("V", v_xhat)] if phases != 2.03 else []))
            sc.dma("gpsimd", wkva_s[:], wkva.rearrange("(k p) d -> p k d", p=128), "LW")
            sc.dma("gpsimd", wqbr_s[:], wqbr.rearrange("(k p) d -> p k d", p=128), "LW")
        lw_b = sc.c("LW")

        if phases in (2.02, 2.03, 2.05):
            _bail(); b_es.close(); pb_es.close(); ab_es.close(); return nc
        for m in range(4 if phases != 2.08 else 1):
            def mmqa(q, m=m):
                ins = None
                for n2 in range(NTQ // 512):
                    for k in range(16):
                        ins = q.matmul(out=PB2[:, n2 * 512:(n2 + 1) * 512],
                                       lhsT=wqa_s[:, k, m * 128:(m + 1) * 128],
                                       rhs=xhat[:, k, n2 * 512:(n2 + 1) * 512],
                                       start=(k == 0), stop=(k == 15))
                return ins
            sc.op("tensor", mmqa, wait=[("LW", lw_b), ("V", sc.c("V"))], inc="T")

            def evqa(q, m=m):
                return q.tensor_scalar(out=qa[:, m, :], in0=PB2[:, :], scalar1=wqab_s[:, m:m + 1],
                                       scalar2=None, op0=ALU.add)
            sc.op("vector", evqa, wait=[("T", sc.c("T"))], inc="V")

        if phases in (2.08, 2.1):
            _bail(); b_es.close(); pb_es.close(); ab_es.close(); return nc
        for m in range(4):
            def mmckv(q, m=m):
                ins = None
                for n2 in range(4):
                    for k in range(16):
                        ins = q.matmul(out=PB1[:, n2 * 512:(n2 + 1) * 512],
                                       lhsT=wkva_s[:, k, m * 128:(m + 1) * 128],
                                       rhs=xhat[:, k, n2 * 512:(n2 + 1) * 512],
                                       start=(k == 0), stop=(k == 15))
                return ins
            sc.op("tensor", mmckv, wait=[("LW", lw_b), ("V", sc.c("V"))], inc="T")

            def evckv(q, m=m):
                return q.tensor_scalar(out=ckv[:, m, :], in0=PB1[:, :], scalar1=wkvab_s[:, m:m + 1],
                                       scalar2=None, op0=ALU.add)
            sc.op("vector", evckv, wait=[("T", sc.c("T"))], inc="V")

        if phases == 2.2:
            _bail(); b_es.close(); pb_es.close(); ab_es.close(); return nc
        for m in range(16):
            def mmkr(q, m=m):
                for k in range(16):
                    q.matmul(out=PB2[:, 0:64], lhsT=xhat[:, k, m * 128:(m + 1) * 128],
                             rhs=wkva_s[:, k, 512:576], start=(k == 0), stop=False)
                return q.matmul(out=PB2[:, 0:64], lhsT=onesrb[:, :], rhs=kpeb_s[:, :],
                                start=False, stop=True)
            sc.op("tensor", mmkr, wait=[("LW", lw_b), ("V", sc.c("V")), ("S", sc.c("S"))], inc="T")

            def evkr(q, m=m):
                return q.activation(krt[:, m, :], PB2[:, 0:64], AF.Copy)
            sc.op("scalar", evkr, wait=[("T", sc.c("T"))], inc="S")

        if phases == 2.3:
            _bail(); b_es.close(); pb_es.close(); ab_es.close(); return nc
        for m in range(16):
            def ropek(q, m=m):
                A = krt[:, m, 0:32]
                B = krt[:, m, 32:64]
                cc = cssb[:, m, 0:32]
                ss = cssb[:, m, 32:64]
                q.tensor_tensor(out=rt[0][:, 0:32], in0=A, in1=cc, op=ALU.mult)
                q.tensor_tensor(out=rt[1][:, 0:32], in0=A, in1=ss, op=ALU.mult)
                q.tensor_tensor(out=rt[2][:, 0:32], in0=B, in1=ss, op=ALU.mult)
                q.tensor_tensor(out=A, in0=rt[0][:, 0:32], in1=rt[2][:, 0:32], op=ALU.subtract)
                q.tensor_tensor(out=rt[2][:, 0:32], in0=B, in1=cc, op=ALU.mult)
                return q.tensor_tensor(out=B, in0=rt[1][:, 0:32], in1=rt[2][:, 0:32], op=ALU.add)
            sc.op("vector", ropek, wait=[("S", sc.c("S")), ("LA", la_small)], inc="V")
        v_ropek = sc.c("V")

        if phases == 2.4:
            _bail(); b_es.close(); pb_es.close(); ab_es.close(); return nc
        for r in range(2):
            def trk(q, r=r):
                ins = None
                for j in range(8):
                    m = r * 8 + j
                    ins = q.transpose(PTB[0:64, j * 128:(j + 1) * 128], krt[:, m, :], idenb[:, :])
                return ins
            sc.op("tensor", trk, wait=[("V", v_ropek), ("LW", sc.c("LW")),
                                       ("S", sc.c("S"))], inc="T")

            def evtrk(q, r=r):
                return q.copy(out=krf[:, r * 1024:(r + 1) * 1024], in_=PTB[0:64, :])
            sc.op("scalar", evtrk, wait=[("T", sc.c("T"))], inc="S")

        if phases == 2.5:
            _bail(); b_es.close(); pb_es.close(); ab_es.close(); return nc
        for m in range(8):
            def mmqr(q, m=m):
                ins = None
                for n2 in range(2):
                    for k in range(4):
                        q.matmul(out=PB2[:, n2 * 512:(n2 + 1) * 512],
                                 lhsT=qa[:, k, m * 128:(m + 1) * 128],
                                 rhs=wqbr_s[:, k, n2 * 512:(n2 + 1) * 512],
                                 start=(k == 0), stop=False)
                    ins = q.matmul(out=PB2[:, n2 * 512:(n2 + 1) * 512], lhsT=onesrb[:, :],
                                   rhs=wqbrb_s[:, n2 * 512:(n2 + 1) * 512], start=False, stop=True)
                return ins
            sc.op("tensor", mmqr, wait=[("V", sc.c("V")), ("S", sc.c("S")), ("LW", lw_b)], inc="T")

            def evqr(q, m=m):
                return q.activation(qrt[:, m, :], PB2[:, :], AF.Copy, scale=QKSCALE)
            sc.op("scalar", evqr, wait=[("T", sc.c("T"))], inc="S")

        if phases == 2.6:
            _bail(); b_es.close(); pb_es.close(); ab_es.close(); return nc
        for m in range(8):
            def ropeq(q, m=m):
                ins = None
                for h in range(16):
                    A = qrt[:, m, h * 64:h * 64 + 32]
                    B = qrt[:, m, h * 64 + 32:h * 64 + 64]
                    cc = cssb[:, m, 0:32]
                    ss = cssb[:, m, 32:64]
                    q.tensor_tensor(out=rt[0][:, 0:32], in0=A, in1=cc, op=ALU.mult)
                    q.tensor_tensor(out=rt[1][:, 0:32], in0=A, in1=ss, op=ALU.mult)
                    q.tensor_tensor(out=rt[2][:, 0:32], in0=B, in1=ss, op=ALU.mult)
                    q.tensor_tensor(out=A, in0=rt[0][:, 0:32], in1=rt[2][:, 0:32], op=ALU.subtract)
                    q.tensor_tensor(out=rt[2][:, 0:32], in0=B, in1=cc, op=ALU.mult)
                    ins = q.tensor_tensor(out=B, in0=rt[1][:, 0:32], in1=rt[2][:, 0:32], op=ALU.add)
                return ins
            sc.op("vector", ropeq, wait=[("S", sc.c("S"))], inc="V")
        v_ropeq = sc.c("V")

        if phases == 2.7:
            _bail(); b_es.close(); pb_es.close(); ab_es.close(); return nc
        for h in range(16):
            def trq(q, h=h):
                ins = None
                for m in range(8):
                    ins = q.transpose(PTB[0:64, m * 128:(m + 1) * 128],
                                      qrt[:, m, h * 64:(h + 1) * 64], idenb[:, :])
                return ins
            sc.op("tensor", trq, wait=[("V", v_ropeq), ("S", sc.c("S"))], inc="T")

            def evtrq(q, h=h):
                return q.copy(out=qrf[:, h, :], in_=PTB[0:64, :])
            sc.op("scalar", evtrq, wait=[("T", sc.c("T"))], inc="S")

        sc.op("vector", lambda q: q.engine_nop(), wait=[("S", sc.c("S"))], inc="V")
        v_phB = sc.c("V")
        if phases <= 2:
            sc.op("vector", lambda q: q.engine_nop(), wait=[("ST", sc.c("ST"))])
            block = es.enter_context(nc.Block())
            sc.emit_all(block, sems)
            b_es.close(); pb_es.close(); ab_es.close()
            return nc
        b_es.close()
        pb_es.close()
        ab_es.close()

        cd_es = ExitStack()
        wkvb_t = [cd_es.enter_context(nc.sbuf_tensor(f"wkvb_t{i}", [128, 4, 256], BF16)) for i in range(2)]
        wqbn_t = [cd_es.enter_context(nc.sbuf_tensor(f"wqbn_t{i}", [128, 4, 128], BF16)) for i in range(2)]
        kn = cd_es.enter_context(nc.sbuf_tensor("kn", [128, NTK], BF16))
        vfm = cd_es.enter_context(nc.sbuf_tensor("vfm", [128, NTK], BF16))
        vt = cd_es.enter_context(nc.sbuf_tensor("vt", [128, 16, 128], BF16))
        qn = cd_es.enter_context(nc.sbuf_tensor("qn", [128, NTQ], BF16))
        expt = [cd_es.enter_context(nc.sbuf_tensor(f"expt{i}", [128, NTQ], BF16)) for i in range(2)]
        rec_sb = cd_es.enter_context(nc.sbuf_tensor("rec_sb", [1, NTQ], F32))
        recb = cd_es.enter_context(nc.sbuf_tensor("recb", [128, NTQ], F32))
        o_sb = cd_es.enter_context(nc.sbuf_tensor("o_sb", [128, 16, NTQ], BF16))
        pc_es = ExitStack()
        P1 = pc_es.enter_context(nc.psum_tensor("P1", [128, NTQ], F32))
        P2 = pc_es.enter_context(nc.psum_tensor("P2", [128, NTQ], F32))
        P3 = pc_es.enter_context(nc.psum_tensor("P3", [128, NTQ], F32))
        PT = pc_es.enter_context(nc.psum_tensor("PT", [128, 512], BF16))

        v_endhead = sc.c("V")
        lw_wk = [0, 0]
        for h in range(16):
            hb = h % 2
            wv = [("V", v_phB)] if h < 2 else [("T", lw_wk[hb])]
            sc.dma("gpsimd", wkvb_t[hb][:],
                   wkvb[:, h * 256:(h + 1) * 256].rearrange("(k p) d -> p k d", p=128),
                   "LW", wait=wv)
            sc.dma("gpsimd", wqbn_t[hb][:],
                   wqbn[:, h * 128:(h + 1) * 128].rearrange("(k p) d -> p k d", p=128),
                   "LW", wait=wv)
            lw_h = sc.c("LW")

            for part, dst, bcol in ((0, kn, 2 * h), (1, vfm, 2 * h + 1)):
                for n2 in range(2):
                    def mmkv(q, part=part, n2=n2, hb=hb):
                        ins = None
                        for n3 in range(2):
                            for k in range(4):
                                ins = q.matmul(
                                    out=P3[:, n3 * 512:(n3 + 1) * 512],
                                    lhsT=wkvb_t[hb][:, k, part * 128:part * 128 + 128],
                                    rhs=ckv[:, k, n2 * 1024 + n3 * 512:n2 * 1024 + (n3 + 1) * 512],
                                    start=(k == 0), stop=(k == 3))
                        return ins
                    sc.op("tensor", mmkv, wait=[("LW", lw_h), ("V", sc.c("V"))], inc="T")

                    def evkv(q, part=part, n2=n2, dst=dst, bcol=bcol):
                        return q.tensor_scalar(out=dst[:, n2 * 1024:(n2 + 1) * 1024], in0=P3[:, :],
                                               scalar1=wkvbb_s[:, bcol:bcol + 1], scalar2=None,
                                               op0=ALU.add)
                    sc.op("vector", evkv, wait=[("T", sc.c("T"))], inc="V")
            v_kv = sc.c("V")

            for g in range(4):
                def trv(q, g=g):
                    ins = None
                    for j in range(4):
                        ins = q.transpose(PT[:, j * 128:(j + 1) * 128],
                                          vfm[:, (g * 4 + j) * 128:(g * 4 + j + 1) * 128],
                                          idenb[:, :])
                    return ins
                sc.op("tensor", trv, wait=[("V", v_kv), ("S", sc.c("S"))], inc="T")

                def evtrv(q, g=g):
                    return q.copy(out=vt[:, g * 4:(g + 1) * 4, :].rearrange("p a b -> p (a b)"),
                                  in_=PT[:, :])
                sc.op("scalar", evtrv, wait=[("T", sc.c("T"))], inc="S")

            def mmqn(q, hb=hb):
                ins = None
                for n2 in range(2):
                    for k in range(4):
                        ins = q.matmul(out=P3[:, n2 * 512:(n2 + 1) * 512],
                                       lhsT=wqbn_t[hb][:, k, :],
                                       rhs=qa[:, k, n2 * 512:(n2 + 1) * 512],
                                       start=(k == 0), stop=(k == 3))
                return ins
            sc.op("tensor", mmqn, wait=[("LW", lw_h), ("V", v_kv)], inc="T")

            def evqn(q, h=h):
                return q.tensor_scalar(out=qn[:, :], in0=P3[:, :], scalar1=wqbnb_s[:, h:h + 1],
                                       scalar2=QKSCALE, op0=ALU.add, op1=ALU.mult)
            sc.op("vector", evqn, wait=[("T", sc.c("T"))], inc="V")
            v_qn = sc.c("V")

            for kt in range(16):
                eb = kt % 2

                def mmsc(q, kt=kt, h=h):
                    ins = None
                    for n2 in range(2):
                        q.matmul(out=P1[:, n2 * 512:(n2 + 1) * 512],
                                 lhsT=kn[:, kt * 128:(kt + 1) * 128],
                                 rhs=qn[:, n2 * 512:(n2 + 1) * 512],
                                 start=True, stop=False)
                        ins = q.matmul(out=P1[:, n2 * 512:(n2 + 1) * 512],
                                       lhsT=krf[:, kt * 128:(kt + 1) * 128],
                                       rhs=qrf[:, h, n2 * 512:(n2 + 1) * 512],
                                       start=False, stop=True)
                    return ins
                w_sc = [("S", sc.c("S"))]
                if kt == 0:
                    w_sc.append(("V", v_qn))
                sc.op("tensor", mmsc, wait=w_sc, inc="T")
                t_sc = sc.c("T")

                def expe(q, kt=kt, eb=eb):
                    return q.activation(expt[eb][:, :], P1[:, :], AF.Exp, bias=m4c[:, 0:1])
                sc.op("scalar", expe, wait=[("T", t_sc)], inc="S")
                s_exp = sc.c("S")

                def mmacc(q, kt=kt, eb=eb):
                    ins = None
                    for n2 in range(2):
                        q.matmul(out=P3[:1, n2 * 512:(n2 + 1) * 512], lhsT=onesb[:, :],
                                 rhs=expt[eb][:, n2 * 512:(n2 + 1) * 512],
                                 start=(kt == 0), stop=(kt == 15))
                    for n2 in range(2):
                        ins = q.matmul(out=P2[:, n2 * 512:(n2 + 1) * 512],
                                       lhsT=vt[:, kt, :],
                                       rhs=expt[eb][:, n2 * 512:(n2 + 1) * 512],
                                       start=(kt == 0), stop=(kt == 15))
                    return ins
                sc.op("tensor", mmacc, wait=[("S", s_exp)], inc="T")
            t_att = sc.c("T")

            def recip(q):
                return q.reciprocal(out=rec_sb[:, :], in_=P3[:1, :])
            sc.op("vector", recip, wait=[("T", t_att)], inc="V")
            v_rec = sc.c("V")

            def mmbc(q):
                ins = None
                for n2 in range(2):
                    ins = q.matmul(out=P3[:, n2 * 512:(n2 + 1) * 512], lhsT=onesr[:, :],
                                   rhs=rec_sb[:, n2 * 512:(n2 + 1) * 512], start=True, stop=True)
                return ins
            sc.op("tensor", mmbc, wait=[("V", v_rec)], inc="T")

            def cprec(q):
                return q.tensor_copy(out=recb[:, :], in_=P3[:, :])
            sc.op("vector", cprec, wait=[("T", sc.c("T"))], inc="V")

            def evo(q, h=h):
                return q.tensor_tensor(out=o_sb[:, h, :], in0=P2[:, :], in1=recb[:, :], op=ALU.mult)
            sc.op("vector", evo, wait=[("V", sc.c("V"))], inc="V")
            v_endhead = sc.c("V")
            lw_wk[hb] = sc.c("T")
        if phases <= 3:
            sc.op("vector", lambda q: q.engine_nop(), wait=[("ST", sc.c("ST"))])
            block = es.enter_context(nc.Block())
            sc.emit_all(block, sems)
            pc_es.close(); cd_es.close()
            return nc
        pc_es.close()

        wo_t = [cd_es.enter_context(nc.sbuf_tensor(f"wo_t{i}", [128, 2048], BF16)) for i in range(2)]
        xr = [cd_es.enter_context(nc.sbuf_tensor(f"xr{i}", [128, 512], F32)) for i in range(2)]
        yt = [cd_es.enter_context(nc.sbuf_tensor(f"yt{i}", [128, 512], F32)) for i in range(2)]
        PY = cd_es.enter_context(nc.psum_tensor("PY", [128, 4096], F32))

        wo_lw = [0, 0]
        ev_done = 0
        for th in range(2):
            for dh in range(2):
                for k in range(16):
                    kb = k % 2
                    sc.dma("gpsimd", wo_t[kb][:, :], wo[k * 128:(k + 1) * 128, :], "LW",
                           wait=[("T", wo_lw[kb])])
                    lw_k = sc.c("LW")

                    def mmo(q, k=k, kb=kb, th=th, dh=dh):
                        ins = None
                        for m in range(8):
                            ins = q.matmul(out=PY[:, m * 512:(m + 1) * 512],
                                           lhsT=wo_t[kb][:, (dh * 8 + m) * 128:(dh * 8 + m) * 128 + 128],
                                           rhs=o_sb[:, k, th * 512:(th + 1) * 512],
                                           start=(k == 0), stop=(k == 15))
                        return ins
                    w = [("LW", lw_k)]
                    if k == 0:
                        w.append(("V", max(v_endhead, ev_done)))
                    sc.op("tensor", mmo, wait=w, inc="T")
                    wo_lw[kb] = sc.c("T")
                t_wo = sc.c("T")
                for m in range(8):
                    mc = dh * 8 + m
                    sc.dma("sync", xr[m % 2][:, :], xseq[mc][:, th * 512:(th + 1) * 512], "LA",
                           wait=[("V", sc.c("V"))])
                    la_x = sc.c("LA")

                    def evy(q, m=m, mc=mc):
                        return q.scalar_tensor_tensor(out=yt[m % 2][:, :],
                                                      in0=PY[:, m * 512:(m + 1) * 512],
                                                      scalar=wob_s[:, mc:mc + 1], in1=xr[m % 2][:, :],
                                                      op0=ALU.add, op1=ALU.add)
                    sc.op("vector", evy, wait=[("T", t_wo), ("LA", la_x), ("ST", sc.c("ST"))], inc="V")
                    sc.dma("sync", y[mc][:, th * 512:(th + 1) * 512], yt[m % 2][:, :], "ST",
                           wait=[("V", sc.c("V"))])
                ev_done = sc.c("V")

        sc.op("vector", lambda q: q.engine_nop(), wait=[("ST", sc.c("ST"))])
        block = es.enter_context(nc.Block())
        sc.emit_all(block, sems)
        block = None
        cd_es.close()
    return nc



def _pad_wgu(w):
    wp = np.zeros((2048, 5632), dtype=w.dtype)
    wp[:, :5504] = w
    return np.ascontiguousarray(wp.reshape(16, 128, 22, 256).transpose(2, 0, 1, 3))


def _pad_wd(w):
    wp = np.zeros((5632, 2048), dtype=w.dtype)
    wp[:5504, :] = w
    return np.ascontiguousarray(wp.reshape(44, 128, 2048))


def materialize_w(inputs):
    nGU, nD = 688, 256
    nc = build_wmat(nGU, nD)
    in_maps = []
    gT = {L: np.ascontiguousarray(inputs[f"L{L}_ffn_gate_idx"].T) for L in (0, 1)}
    uT = {L: np.ascontiguousarray(inputs[f"L{L}_ffn_up_idx"].T) for L in (0, 1)}
    dT = {L: np.ascontiguousarray(inputs[f"L{L}_ffn_down_idx"].T) for L in (0, 1)}
    lutb = {L: np.ascontiguousarray(np.broadcast_to(
        np.asarray(inputs[f"L{L}_ffn_lut"], dtype=np.float32), (128, 16))) for L in (0, 1)}
    for c in range(8):
        m = {"lut0": lutb[0], "lut1": lutb[1]}
        for L in (0, 1):
            m[f"idx_L{L}g"] = np.ascontiguousarray(gT[L][:, c * nGU:(c + 1) * nGU])
            m[f"idx_L{L}u"] = np.ascontiguousarray(uT[L][:, c * nGU:(c + 1) * nGU])
            m[f"idx_L{L}d"] = np.ascontiguousarray(dT[L][:, c * nD:(c + 1) * nD])
        in_maps.append(m)
    res = run_bass_kernel_spmd(nc, in_maps, core_ids=list(range(8)))
    out = {}
    for L in (0, 1):
        wgf = np.concatenate([res.results[c][f"w_L{L}g"] for c in range(8)], axis=1)
        wuf = np.concatenate([res.results[c][f"w_L{L}u"] for c in range(8)], axis=1)
        wdf = np.concatenate([res.results[c][f"w_L{L}d"] for c in range(8)], axis=1)
        out[L] = (_pad_wgu(wgf), _pad_wgu(wuf), _pad_wd(wdf))
    return out


def run_ffn(xT, wtup, o_fm=None, woT=None):
    wg_t, wu_t, wd_t = wtup
    nc = build_ffn(NT=1024, wo_prologue=o_fm is not None)
    in_maps = []
    for c in range(8):
        m = {"xin": np.ascontiguousarray(xT[:, c * 1024:(c + 1) * 1024]).reshape(16, 128, 1024),
             "wg": wg_t, "wu": wu_t, "wd": wd_t}
        if o_fm is not None:
            m["o_fm"] = np.ascontiguousarray(o_fm[:, c * 1024:(c + 1) * 1024]).reshape(12, 128, 1024)
            m["woT"] = np.ascontiguousarray(woT).reshape(12, 128, 2048)
        in_maps.append(m)
    res = run_bass_kernel_spmd(nc, in_maps, core_ids=list(range(8)))
    return np.concatenate([res.results[c]["y"].reshape(2048, 1024) for c in range(8)], axis=1)


def mla_host_inputs(inputs):
    perm = np.concatenate([np.arange(0, 64, 2), np.arange(1, 64, 2)])
    wqa = np.ascontiguousarray(inputs["L0_attn_wq_a_w"], dtype=np.float32)
    wqab = np.ascontiguousarray(np.asarray(inputs["L0_attn_wq_a_b"], dtype=np.float32).reshape(4, 128).T)
    wqb = np.asarray(inputs["L0_attn_wq_b_w"], dtype=np.float32)
    wqbb = np.asarray(inputs["L0_attn_wq_b_b"], dtype=np.float32)
    wqbn = np.concatenate([wqb[:, h * 192:h * 192 + 128] for h in range(16)], axis=1)
    wqbr = np.concatenate([wqb[:, h * 192 + 128:h * 192 + 192][:, perm] for h in range(16)], axis=1)
    wqbnb = np.stack([wqbb[h * 192:h * 192 + 128] for h in range(16)], axis=1)
    wqbrb = np.concatenate([wqbb[h * 192 + 128:h * 192 + 192][perm] for h in range(16)])[None, :]
    wkva_full = np.asarray(inputs["L0_attn_wkv_a_w"], dtype=np.float32).copy()
    wkva_full[:, 512:576] = wkva_full[:, 512:576][:, perm]
    wkvab_full = np.asarray(inputs["L0_attn_wkv_a_b"], dtype=np.float32)
    wkvab = np.ascontiguousarray(wkvab_full[:512].reshape(4, 128).T)
    kpeb = np.ascontiguousarray(wkvab_full[512:576][perm][None, :])
    wkvb = np.ascontiguousarray(inputs["L0_attn_wkv_b_w"], dtype=np.float32)
    wkvbb = np.ascontiguousarray(
        np.asarray(inputs["L0_attn_wkv_b_b"], dtype=np.float32).reshape(16, 2, 128).transpose(2, 0, 1).reshape(128, 32))
    wo = np.ascontiguousarray(inputs["L0_attn_wo_w"], dtype=np.float32)
    wob = np.ascontiguousarray(np.asarray(inputs["L0_attn_wo_b"], dtype=np.float32).reshape(16, 128).T)
    inv_freq = (1.0 / (1e6 ** (np.arange(0, 64, 2, dtype=np.float64) / 64.0)))
    t = np.arange(2048, dtype=np.float64)
    freqs = np.outer(t, inv_freq)
    cs = np.concatenate([np.cos(freqs), np.sin(freqs)], axis=1).astype(np.float32)
    cs_t = np.ascontiguousarray(cs.reshape(16, 128, 64))
    iden = np.eye(128, dtype=np.float32)

    x0 = np.asarray(inputs["L0_attn_x"], dtype=np.float32)
    maps = []
    for c in range(8):
        b, half = c // 2, c % 2
        xT = np.ascontiguousarray(x0[b].T)
        if half == 1:
            xT = np.ascontiguousarray(np.concatenate([xT[:, 1024:], xT[:, :1024]], axis=1))
            cs_c = np.ascontiguousarray(np.concatenate([cs_t[8:], cs_t[:8]], axis=0))
        else:
            cs_c = cs_t
        maps.append(dict(xseq=xT.reshape(16, 128, 2048), wqa=wqa, wqab=wqab,
                         wqbn=np.ascontiguousarray(wqbn), wqbnb=np.ascontiguousarray(wqbnb),
                         wqbr=np.ascontiguousarray(wqbr), wqbrb=np.ascontiguousarray(wqbrb),
                         wkva=wkva_full, wkvab=wkvab, kpeb=kpeb, wkvb=wkvb, wkvbb=wkvbb,
                         wo=wo, wob=wob, cs=cs_c, iden=iden))
    return maps


def run_mla(inputs):
    nc = build_mla()
    maps = mla_host_inputs(inputs)
    res = run_bass_kernel_spmd(nc, maps, core_ids=list(range(8)))
    xT_out = np.empty((2048, 8192), np.float32)
    for c in range(8):
        b, half = c // 2, c % 2
        xT_out[:, b * 2048 + half * 1024: b * 2048 + half * 1024 + 1024] = \
            res.results[c]["y"].reshape(2048, 1024)
    return xT_out


def kernel(**inputs):
    W = materialize_w(inputs)
    xT = run_mla(inputs)
    xT = run_ffn(xT, W[0])
    o = run_gdn(inputs, xT)
    o_fm = np.ascontiguousarray(o.T)
    woT = np.ascontiguousarray(np.asarray(inputs["L1_attn_Wo"], dtype=np.float32).T)
    xT = run_ffn(xT, W[1], o_fm=o_fm, woT=woT)
    out = np.ascontiguousarray(xT.T).reshape(4, 2048, 2048).astype(np.float32)
    return out



def build_gdn(n_tiles=16, n_heads=3, dbg=False):
    NT = 2048
    nc = bass.Bass("TRN2", target_bir_lowering=False, detect_race_conditions=False)
    xseq = nc.dram_tensor("xseq", [16, 128, NT], F32, kind="ExternalInput").ap()
    wqT = nc.dram_tensor("wqT", [2048, 768], F32, kind="ExternalInput").ap()
    wkT = nc.dram_tensor("wkT", [2048, 768], F32, kind="ExternalInput").ap()
    wvT = nc.dram_tensor("wvT", [2048, 768], F32, kind="ExternalInput").ap()
    wgT = nc.dram_tensor("wgT", [2048, 768], F32, kind="ExternalInput").ap()
    wbaT = nc.dram_tensor("wbaT", [2048, 6], F32, kind="ExternalInput").ap()
    convq = nc.dram_tensor("convq", [768, 4], F32, kind="ExternalInput").ap()
    convk = nc.dram_tensor("convk", [768, 4], F32, kind="ExternalInput").ap()
    convv = nc.dram_tensor("convv", [768, 4], F32, kind="ExternalInput").ap()
    alog = nc.dram_tensor("alog", [1, 3], F32, kind="ExternalInput").ap()
    dtb = nc.dram_tensor("dtb", [1, 3], F32, kind="ExternalInput").ap()
    onw = nc.dram_tensor("onw", [1, 256], F32, kind="ExternalInput").ap()
    iden = nc.dram_tensor("iden", [128, 128], F32, kind="ExternalInput").ap()
    mskLs = nc.dram_tensor("mskLs", [128, 128], F32, kind="ExternalInput").ap()
    mskUi = nc.dram_tensor("mskUi", [128, 128], F32, kind="ExternalInput").ap()
    triu1 = nc.dram_tensor("triu1", [128, 128], F32, kind="ExternalInput").ap()
    o_out = nc.dram_tensor("o_out", [16, 128, 768], BF16, kind="ExternalOutput").ap()
    if dbg:
        dbgS = nc.dram_tensor("dbgS", [48, 128, 512], F32, kind="ExternalOutput").ap()
        dbgP = nc.dram_tensor("dbgP", [48, 128, 512], F32, kind="ExternalOutput").ap()
        dbgU = nc.dram_tensor("dbgU", [48, 128, 256], F32, kind="ExternalOutput").ap()
        dbgK = nc.dram_tensor("dbgK", [48, 128, 256], F32, kind="ExternalOutput").ap()
        dbgK2 = nc.dram_tensor("dbgK2", [48, 128, 387], F32, kind="ExternalOutput").ap()
        dbgRows = nc.dram_tensor("dbgRows", [128, 246], F32, kind="ExternalOutput").ap()
        dbgQ = nc.dram_tensor("dbgQ", [6, 128, 2048], BF16, kind="ExternalOutput").ap()
        dbgKk = nc.dram_tensor("dbgKk", [6, 128, 2048], BF16, kind="ExternalOutput").ap()
        dbgV = nc.dram_tensor("dbgV", [6, 128, 2048], BF16, kind="ExternalOutput").ap()
        dbgG = nc.dram_tensor("dbgG", [16, 128, 768], BF16, kind="ExternalOutput").ap()


    with ExitStack() as es:
        identf = es.enter_context(nc.sbuf_tensor("identf", [128, 128], F32))
        idenb = es.enter_context(nc.sbuf_tensor("idenb", [128, 128], BF16))
        mLs = es.enter_context(nc.sbuf_tensor("mLs", [128, 128], F32))
        mUi = es.enter_context(nc.sbuf_tensor("mUi", [128, 128], F32))
        tri = es.enter_context(nc.sbuf_tensor("tri", [128, 128], F32))
        ones = es.enter_context(nc.sbuf_tensor("ones", [128, 1], F32))
        onesr = es.enter_context(nc.sbuf_tensor("onesr", [1, 128], F32))
        epsc = es.enter_context(nc.sbuf_tensor("epsc", [128, 1], F32))
        eps6 = es.enter_context(nc.sbuf_tensor("eps6", [128, 1], F32))
        onec = es.enter_context(nc.sbuf_tensor("onec", [128, 1], F32))
        alog_s = es.enter_context(nc.sbuf_tensor("alog_s", [1, 3], F32))
        dtb_s = es.enter_context(nc.sbuf_tensor("dtb_s", [1, 3], F32))
        onw_s = es.enter_context(nc.sbuf_tensor("onw_s", [1, 256], F32))
        wnb = es.enter_context(nc.sbuf_tensor("wnb", [128, 256], F32))
        negA = es.enter_context(nc.sbuf_tensor("negA", [128, 3], F32))
        dtbb = es.enter_context(nc.sbuf_tensor("dtbb", [128, 3], F32))
        bgp = es.enter_context(nc.sbuf_tensor("bgp", [128, 16, 6], F32))
        beta = es.enter_context(nc.sbuf_tensor("beta", [128, 16, 3], F32))
        gg = es.enter_context(nc.sbuf_tensor("gg", [128, 16, 3], F32))
        cumg = es.enter_context(nc.sbuf_tensor("cumg", [128, 16, 3], F32))
        ecolb = es.enter_context(nc.sbuf_tensor("ecolb", [128, 16, 3], F32))
        acolb = es.enter_context(nc.sbuf_tensor("acolb", [128, 16, 3], F32))
        qv_es = ExitStack()
        q_sb = qv_es.enter_context(nc.sbuf_tensor("q_sb", [128, 6, NT], BF16))
        k_sb = qv_es.enter_context(nc.sbuf_tensor("k_sb", [128, 6, NT], BF16))
        v_sb = qv_es.enter_context(nc.sbuf_tensor("v_sb", [128, 6, NT], BF16))
        gate = qv_es.enter_context(nc.sbuf_tensor("gate", [128, 16, 768], BF16))

        sems = {n: es.enter_context(nc.semaphore(n)) for n in
                ("LA", "LW", "ST", "V", "S", "T", "G")}
        sc = Sched()

        for t_, v_ in ((ones, 1.0), (onesr, 1.0), (epsc, 1e-5), (eps6, 1e-6), (onec, 1.0)):
            sc.op("gpsimd", (lambda t, v: lambda q: q.memset(t[:], v))(t_, v_), inc="G")
        G0 = sc.c("G")
        for dst, src in ((identf, iden), (mLs, mskLs), (mUi, mskUi), (tri, triu1),
                         (alog_s, alog), (dtb_s, dtb), (onw_s, onw)):
            sc.dma("sync", dst[:], src[:], "LA")
        sc.dma("gpsimd", idenb[:], iden[:, :], "LW")
        la_small = sc.c("LA")

        ab_es = ExitStack()
        xhat = ab_es.enter_context(nc.sbuf_tensor("xhat", [128, 16, NT], BF16))
        a_es = ExitStack()
        xt = [a_es.enter_context(nc.sbuf_tensor(f"xt{i}", [128, 1024], F32)) for i in range(2)]
        sq = a_es.enter_context(nc.sbuf_tensor("sq", [128, 1024], F32))
        mrow = a_es.enter_context(nc.sbuf_tensor("mrow", [1, NT], F32))
        rrow = a_es.enter_context(nc.sbuf_tensor("rrow", [1, NT], F32))
        pa_es = ExitStack()
        PA = pa_es.enter_context(nc.psum_tensor("PA", [128, NT], F32))
        PB = pa_es.enter_context(nc.psum_tensor("PB", [128, NT], F32))

        for c32 in range(32):
            fc, hf = c32 // 2, c32 % 2
            sc.dma("sync", xt[c32 % 2][:, :], xseq[fc][:, hf * 1024:(hf + 1) * 1024], "LA",
                   wait=[("T", sc.c("T"))])
            la_here = sc.c("LA")
            sc.op("scalar", (lambda c32: lambda q: q.activation(sq[:, :], xt[c32 % 2][:, :],
                                                                AF.Square))(c32),
                  wait=[("LA", la_here), ("T", sc.c("T"))], inc="S")
            s_here = sc.c("S")

            def mmstat(q, c32=c32, fc=fc, hf=hf):
                ins = None
                for n2 in range(2):
                    o0 = hf * 1024 + n2 * 512
                    q.matmul(out=PA[:1, o0:o0 + 512], lhsT=ones[:, :],
                             rhs=xt[c32 % 2][:, n2 * 512:(n2 + 1) * 512],
                             start=(fc == 0), stop=(fc == 15))
                for n2 in range(2):
                    o0 = hf * 1024 + n2 * 512
                    ins = q.matmul(out=PB[:1, o0:o0 + 512], lhsT=ones[:, :],
                                   rhs=sq[:, n2 * 512:(n2 + 1) * 512],
                                   start=(fc == 0), stop=(fc == 15))
                return ins
            sc.op("tensor", mmstat, wait=[("S", s_here), ("G", G0)], inc="T")
        t_stats = sc.c("T")

        def rows(q):
            q.activation(mrow[:, :], PA[:1, :], AF.Copy, scale=1.0 / 2048.0)
            return q.activation(rrow[:, :], PB[:1, :], AF.Copy, scale=1.0 / 2048.0)
        sc.op("scalar", rows, wait=[("T", t_stats)], inc="S")
        s_r1 = sc.c("S")

        def bcmean(q):
            ins = None
            for n2 in range(4):
                ins = q.matmul(out=PA[:, n2 * 512:(n2 + 1) * 512], lhsT=onesr[:, :],
                               rhs=mrow[0:1, n2 * 512:(n2 + 1) * 512], start=True, stop=True)
            return ins
        sc.op("tensor", bcmean, wait=[("S", s_r1)], inc="T")
        t_bcm = sc.c("T")

        def rows2(q):
            return q.activation(mrow[:, :], mrow[:, :], AF.Square)
        sc.op("scalar", rows2, wait=[("T", t_bcm)], inc="S")

        sc.op("vector", lambda q: q.tensor_tensor(out=rrow[:, :], in0=rrow[:, :], in1=mrow[:, :],
                                                  op=ALU.subtract),
              wait=[("S", sc.c("S"))], inc="V")

        def rows3(q):
            q.activation(rrow[:, :], rrow[:, :], AF.Ln, bias=epsc[0:1, 0:1])
            return q.activation(rrow[:, :], rrow[:, :], AF.Exp, scale=-0.5)
        sc.op("scalar", rows3, wait=[("V", sc.c("V")), ("G", G0)], inc="S")
        s_rows = sc.c("S")

        def bcstd(q):
            ins = None
            for n2 in range(4):
                ins = q.matmul(out=PB[:, n2 * 512:(n2 + 1) * 512], lhsT=onesr[:, :],
                               rhs=rrow[0:1, n2 * 512:(n2 + 1) * 512], start=True, stop=True)
            return ins
        sc.op("tensor", bcstd, wait=[("S", s_rows)], inc="T")
        t_bc = sc.c("T")

        for c32 in range(32):
            fc, hf = c32 // 2, c32 % 2
            sc.dma("sync", xt[c32 % 2][:, :], xseq[fc][:, hf * 1024:(hf + 1) * 1024], "LA",
                   wait=[("T", t_bc), ("V", sc.c("V"))])
            la_here = sc.c("LA")

            def apply(q, c32=c32, fc=fc, hf=hf):
                q.tensor_tensor(out=xt[c32 % 2][:, :], in0=xt[c32 % 2][:, :],
                                in1=PA[:, hf * 1024:(hf + 1) * 1024], op=ALU.subtract)
                return q.tensor_tensor(out=xhat[:, fc, hf * 1024:(hf + 1) * 1024],
                                       in0=xt[c32 % 2][:, :],
                                       in1=PB[:, hf * 1024:(hf + 1) * 1024], op=ALU.mult)
            sc.op("vector", apply, wait=[("LA", la_here), ("T", t_bc)], inc="V")
        v_xhat = sc.c("V")
        a_es.close()
        pa_es.close()

        b_es = ExitStack()
        wtile = [b_es.enter_context(nc.sbuf_tensor(f"wtile{i}", [128, 16, 128], BF16)) for i in range(2)]
        wba_s = b_es.enter_context(nc.sbuf_tensor("wba_s", [128, 16, 6], BF16))
        pre = b_es.enter_context(nc.sbuf_tensor("pre", [128, 2051], F32))
        cacc = b_es.enter_context(nc.sbuf_tensor("cacc", [128, 2048], F32))
        cvw = b_es.enter_context(nc.sbuf_tensor("cvw", [128, 6, 4], F32))
        srow = b_es.enter_context(nc.sbuf_tensor("srow", [1, NT], F32))
        pb_es = ExitStack()
        PP = pb_es.enter_context(nc.psum_tensor("PP", [128, NT], F32))
        PQ = pb_es.enter_context(nc.psum_tensor("PQ", [128, NT], F32))

        sc.op("gpsimd", lambda q: q.memset(pre[:, 0:3], 0.0), wait=[("V", v_xhat)], inc="G")
        Gpre = sc.c("G")
        sc.dma("gpsimd", wba_s[:], wbaT.rearrange("(k p) d -> p k d", p=128), "LW",
               wait=[("V", v_xhat)])
        lw_bg = sc.c("LW")

        for tt in range(16):
            def mmba(q, tt=tt):
                ins = None
                for k in range(16):
                    ins = q.matmul(out=PQ[:, 0:6], lhsT=xhat[:, k, tt * 128:(tt + 1) * 128],
                                   rhs=wba_s[:, k, :], start=(k == 0), stop=(k == 15))
                return ins
            sc.op("tensor", mmba, wait=[("LW", lw_bg), ("V", v_xhat), ("S", sc.c("S"))], inc="T")

            def evba(q, tt=tt):
                return q.activation(bgp[:, tt, :], PQ[:, 0:6], AF.Copy)
            sc.op("scalar", evba, wait=[("T", sc.c("T"))], inc="S")

        for cc in range(6):
            b2 = cc % 2
            sc.dma("gpsimd", wtile[b2][:],
                   wgT[:, cc * 128:(cc + 1) * 128].rearrange("(k p) d -> p k d", p=128),
                   "LW", wait=[("T", sc.c("T")), ("V", v_xhat)])
            lw_cc = sc.c("LW")
            for tt in range(16):
                def mmgate(q, tt=tt, b2=b2):
                    ins = None
                    for k in range(16):
                        ins = q.matmul(out=PQ[:, (tt % 4) * 128:(tt % 4) * 128 + 128],
                                       lhsT=xhat[:, k, tt * 128:(tt + 1) * 128],
                                       rhs=wtile[b2][:, k, :], start=(k == 0), stop=(k == 15))
                    return ins
                sc.op("tensor", mmgate, wait=[("LW", lw_cc), ("S", sc.c("S"))], inc="T")

                def evgate(q, tt=tt, cc=cc):
                    return q.activation(gate[:, tt, cc * 128:(cc + 1) * 128],
                                        PQ[:, (tt % 4) * 128:(tt % 4) * 128 + 128], AF.Silu)
                sc.op("scalar", evgate, wait=[("T", sc.c("T"))], inc="S")

        for ti, (wsrc, cvsrc, dst, is_v) in enumerate(
                ((wqT, convq, q_sb, False), (wkT, convk, k_sb, False), (wvT, convv, v_sb, True))):
            sc.dma("sync", cvw[:, :, :], cvsrc.rearrange("(c p) i -> p c i", p=128), "LA",
                   wait=[("V", sc.c("V")), ("S", sc.c("S"))])
            la_cv = sc.c("LA")
            for hh in range(3):
                for e in range(2):
                    ch = hh * 2 + e
                    b2 = ch % 2
                    sc.dma("gpsimd", wtile[b2][:],
                           wsrc[:, ch * 128:(ch + 1) * 128].rearrange("(k p) d -> p k d", p=128),
                           "LW", wait=[("T", sc.c("T")), ("V", v_xhat)])
                    lw_h = sc.c("LW")

                    def mmproj(q, b2=b2):
                        ins = None
                        for n2 in range(4):
                            for k in range(16):
                                ins = q.matmul(out=PP[:, n2 * 512:(n2 + 1) * 512],
                                               lhsT=wtile[b2][:, k, :],
                                               rhs=xhat[:, k, n2 * 512:(n2 + 1) * 512],
                                               start=(k == 0), stop=(k == 15))
                        return ins
                    sc.op("tensor", mmproj, wait=[("LW", lw_h), ("V", sc.c("V")),
                                                  ("S", sc.c("S"))], inc="T")
                    t_proj = sc.c("T")

                    def evpre(q):
                        return q.tensor_copy(out=pre[:, 3:2051], in_=PP[:, :])
                    sc.op("vector", evpre, wait=[("T", t_proj), ("G", Gpre), ("LA", la_cv),
                                                 ("S", sc.c("S"))], inc="V")
                    v_pre = sc.c("V")

                    def conv(q, ch=ch):
                        q.tensor_scalar(out=cacc[:, :], in0=pre[:, 3:2051], scalar1=cvw[:, ch, 3:4],
                                        scalar2=None, op0=ALU.mult)
                        last = None
                        for i in range(3):
                            last = q.scalar_tensor_tensor(out=cacc[:, :], in0=pre[:, i:i + 2048],
                                                          scalar=cvw[:, ch, i:i + 1], in1=cacc[:, :],
                                                          op0=ALU.mult, op1=ALU.add)
                        return last
                    sc.op("vector", conv, wait=[("V", v_pre)], inc="V")
                    v_conv = sc.c("V")

                    def silu_post(q, ch=ch, dst=dst):
                        return q.activation(dst[:, ch, :], cacc[:, :], AF.Silu)
                    sc.op("scalar", silu_post, wait=[("V", v_conv)], inc="S")

                if not is_v:
                    for e in range(2):
                        ch = hh * 2 + e

                        def sqp(q, ch=ch, dst=dst):
                            return q.activation(pre[:, 3:2051], dst[:, ch, :], AF.Square)
                        sc.op("scalar", sqp, wait=[("S", sc.c("S")), ("T", sc.c("T")),
                                                   ("V", sc.c("V"))], inc="S")
                        s_sq = sc.c("S")

                        def mmsum(q, e=e):
                            ins = None
                            for n2 in range(4):
                                ins = q.matmul(out=PQ[:1, n2 * 512:(n2 + 1) * 512], lhsT=ones[:, :],
                                               rhs=pre[:, 3 + n2 * 512:3 + (n2 + 1) * 512],
                                               start=(e == 0), stop=(e == 1))
                            return ins
                        sc.op("tensor", mmsum, wait=[("S", s_sq)], inc="T")
                    t_sum = sc.c("T")

                    def rstd_(q):
                        q.activation(srow[:, :], PQ[:1, :], AF.Ln, bias=eps6[0:1, 0:1])
                        return q.activation(srow[:, :], srow[:, :], AF.Exp, scale=-0.5)
                    sc.op("scalar", rstd_, wait=[("T", t_sum)], inc="S")
                    if ti == 0:
                        sc.op("scalar", lambda q: q.mul(out=srow[:, :], in_=srow[:, :], mul=1.0 / 16.0),
                              wait=[("S", sc.c("S"))], inc="S")
                    s_rst = sc.c("S")

                    def bcr(q):
                        ins = None
                        for n2 in range(4):
                            ins = q.matmul(out=PQ[:, n2 * 512:(n2 + 1) * 512], lhsT=onesr[:, :],
                                           rhs=srow[0:1, n2 * 512:(n2 + 1) * 512],
                                           start=True, stop=True)
                        return ins
                    sc.op("tensor", bcr, wait=[("S", s_rst)], inc="T")
                    t_bcr = sc.c("T")

                    for e in range(2):
                        ch = hh * 2 + e

                        def scl(q, ch=ch, dst=dst):
                            return q.tensor_tensor(out=dst[:, ch, :], in0=dst[:, ch, :],
                                                   in1=PQ[:, :], op=ALU.mult)
                        sc.op("vector", scl, wait=[("T", t_bcr), ("V", sc.c("V"))], inc="V")

        def bc3(q):
            q.matmul(out=PQ[:, 0:3], lhsT=onesr[:, :], rhs=alog_s[:, :], start=True, stop=True)
            q.matmul(out=PQ[:, 512:515], lhsT=onesr[:, :], rhs=dtb_s[:, :], start=True, stop=True)
            return q.matmul(out=PQ[:, 1024:1280], lhsT=onesr[:, :], rhs=onw_s[:, :],
                            start=True, stop=True)
        sc.op("tensor", bc3, wait=[("V", sc.c("V")), ("LA", la_small), ("G", G0),
                                   ("S", sc.c("S"))], inc="T")
        t_bc3 = sc.c("T")

        def evbc3(q):
            q.activation(negA[:, :], PQ[:, 0:3], AF.Exp)
            q.copy(out=dtbb[:, :], in_=PQ[:, 512:515])
            return q.copy(out=wnb[:, :], in_=PQ[:, 1024:1280])
        sc.op("scalar", evbc3, wait=[("T", t_bc3)], inc="S")

        def negA_(q):
            return q.tensor_scalar(out=negA[:, :], in0=negA[:, :], scalar1=-1.0, scalar2=None,
                                   op0=ALU.mult)
        sc.op("vector", negA_, wait=[("S", sc.c("S"))], inc="V")
        v_negA = sc.c("V")

        for tt in range(16):
            def sigm(q, tt=tt):
                return q.activation(beta[:, tt, :], bgp[:, tt, 0:3], AF.Sigmoid)
            sc.op("scalar", sigm, wait=[("V", v_negA)], inc="S")

            def gcalc_a(q, tt=tt):
                return q.tensor_tensor(out=gg[:, tt, :], in0=bgp[:, tt, 3:6], in1=dtbb[:, :],
                                       op=ALU.add)
            sc.op("vector", gcalc_a, wait=[("S", sc.c("S")), ("V", sc.c("V"))], inc="V")

            def gcalc_b(q, tt=tt):
                return q.tensor_scalar(out=pre[:, 0:3], in0=gg[:, tt, :], scalar1=0.0,
                                       scalar2=None, op0=ALU.max)
            sc.op("vector", gcalc_b, wait=[("V", sc.c("V"))], inc="V")

            def gact(q, tt=tt):
                q.activation(pre[:, 4:7], gg[:, tt, :], AF.Abs)
                q.activation(pre[:, 8:11], pre[:, 4:7], AF.Exp, scale=-1.0)
                return q.activation(pre[:, 8:11], pre[:, 8:11], AF.Ln, bias=onec[:, 0:1])
            sc.op("scalar", gact, wait=[("V", sc.c("V"))], inc="S")

            def gfin_a(q, tt=tt):
                return q.tensor_tensor(out=pre[:, 0:3], in0=pre[:, 0:3], in1=pre[:, 8:11],
                                       op=ALU.add)
            sc.op("vector", gfin_a, wait=[("S", sc.c("S"))], inc="V")

            def gfin_b(q, tt=tt):
                return q.tensor_tensor(out=gg[:, tt, :], in0=pre[:, 0:3], in1=negA[:, :],
                                       op=ALU.mult)
            sc.op("vector", gfin_b, wait=[("V", sc.c("V"))], inc="V")
            v_g = sc.c("V")

            def mmcum(q, tt=tt):
                return q.matmul(out=PQ[:, 8:11], lhsT=tri[:, :], rhs=gg[:, tt, :],
                                start=True, stop=True)
            sc.op("tensor", mmcum, wait=[("V", v_g)], inc="T")

            def evcum(q, tt=tt):
                return q.copy(out=cumg[:, tt, :], in_=PQ[:, 8:11])
            sc.op("scalar", evcum, wait=[("T", sc.c("T"))], inc="S")

            def mmrs(q, tt=tt):
                return q.matmul(out=PQ[:, 16:19], lhsT=mLs[:, :], rhs=gg[:, tt, :],
                                start=True, stop=True)
            sc.op("tensor", mmrs, wait=[("S", sc.c("S"))], inc="T")

            def evrs(q, tt=tt):
                return q.activation(ecolb[:, tt, :], PQ[:, 16:19], AF.Exp, scale=-1.0)
            sc.op("scalar", evrs, wait=[("T", sc.c("T"))], inc="S")

            def acal(q, tt=tt):
                return q.tensor_tensor(out=acolb[:, tt, :], in0=cumg[:, tt, :],
                                       in1=PQ[:, 16:19], op=ALU.subtract)
            sc.op("vector", acal, wait=[("S", sc.c("S"))], inc="V")

            def acal2(q, tt=tt):
                return q.activation(acolb[:, tt, :], acolb[:, tt, :], AF.Exp)
            sc.op("scalar", acal2, wait=[("V", sc.c("V"))], inc="S")
        s_prep = sc.c("S")
        v_prep = sc.c("V")

        sc.op("vector", lambda q: q.engine_nop(), wait=[("S", s_prep), ("T", sc.c("T"))], inc="V")
        v_phB = sc.c("V")
        b_es.close()
        pb_es.close()
        ab_es.close()

        c_es = ExitStack()
        S0 = [c_es.enter_context(nc.sbuf_tensor(f"S0_{h}", [128, 2, 256], F32)) for h in range(3)]
        S1 = [c_es.enter_context(nc.sbuf_tensor(f"S1_{h}", [128, 2, 256], F32)) for h in range(3)]
        Sbf = c_es.enter_context(nc.sbuf_tensor("Sbf", [128, 2, 256], BF16))
        crow = c_es.enter_context(nc.sbuf_tensor("crow", [1, 128], F32))
        arow = c_es.enter_context(nc.sbuf_tensor("arow", [1, 128], F32))
        ccol = c_es.enter_context(nc.sbuf_tensor("ccol", [128, 1], F32))
        accol = c_es.enter_context(nc.sbuf_tensor("accol", [128, 1], F32))
        dT = c_es.enter_context(nc.sbuf_tensor("dT", [128, 128], F32))
        dU = c_es.enter_context(nc.sbuf_tensor("dU", [128, 128], F32))
        kk = c_es.enter_context(nc.sbuf_tensor("kk", [128, 128], F32))
        kq = c_es.enter_context(nc.sbuf_tensor("kq", [128, 128], F32))
        Nn = c_es.enter_context(nc.sbuf_tensor("Nn", [128, 128], BF16))
        Np = c_es.enter_context(nc.sbuf_tensor("Np", [128, 128], BF16))
        Xc = c_es.enter_context(nc.sbuf_tensor("Xc", [128, 128], BF16))
        Xp = c_es.enter_context(nc.sbuf_tensor("Xp", [128, 128], BF16))
        Tp = c_es.enter_context(nc.sbuf_tensor("Tp", [128, 128], BF16))
        Tpf = c_es.enter_context(nc.sbuf_tensor("Tpf", [128, 128], F32))
        pp_sb = c_es.enter_context(nc.sbuf_tensor("pp_sb", [128, 128], BF16))
        kd = c_es.enter_context(nc.sbuf_tensor("kd", [128, 2, 128], BF16))
        qd = c_es.enter_context(nc.sbuf_tensor("qd", [128, 2, 128], BF16))
        ktm = c_es.enter_context(nc.sbuf_tensor("ktm", [128, 256], BF16))
        kptm = c_es.enter_context(nc.sbuf_tensor("kptm", [128, 256], BF16))
        vtm = c_es.enter_context(nc.sbuf_tensor("vtm", [128, 256], BF16))
        Rt = c_es.enter_context(nc.sbuf_tensor("Rt", [128, 256], BF16))
        Ub = c_es.enter_context(nc.sbuf_tensor("Ub", [128, 256], BF16))
        onr = c_es.enter_context(nc.sbuf_tensor("onr", [128, 256], F32))
        osc = c_es.enter_context(nc.sbuf_tensor("osc", [128, 256], F32))
        oout = c_es.enter_context(nc.sbuf_tensor("oout", [128, 256], BF16))
        pc_es = ExitStack()
        PB0 = pc_es.enter_context(nc.psum_tensor("PB0", [128, 512], F32))
        PB1 = pc_es.enter_context(nc.psum_tensor("PB1", [128, 512], F32))
        PB2 = pc_es.enter_context(nc.psum_tensor("PB2", [128, 512], F32))
        PB3 = pc_es.enter_context(nc.psum_tensor("PB3", [128, 512], F32))
        PB4 = pc_es.enter_context(nc.psum_tensor("PB4", [128, 512], F32))
        PB5 = pc_es.enter_context(nc.psum_tensor("PB5", [128, 512], F32))
        PTb = pc_es.enter_context(nc.psum_tensor("PTb", [128, 512], BF16))
        PR1 = pc_es.enter_context(nc.psum_tensor("PR1", [1, 128], F32))

        if dbg:
            def dmprows(q):
                return q.tensor_copy(out=pre[:, 0:2048][:, 0:192].rearrange("p (a b) -> p a b", b=48)
                                     if False else pre[:, 0:192], in_=pre[:, 0:192])
            sc.dma("sync", dbgRows[:, 0:48], cumg[:, :, :].rearrange("p a b -> p (a b)"), "ST",
                   wait=[("V", v_phB)])
            sc.dma("sync", dbgRows[:, 48:96], ecolb[:, :, :].rearrange("p a b -> p (a b)"), "ST")
            sc.dma("sync", dbgRows[:, 96:144], acolb[:, :, :].rearrange("p a b -> p (a b)"), "ST")
            sc.dma("sync", dbgRows[:, 144:192], beta[:, :, :].rearrange("p a b -> p (a b)"), "ST")
            sc.dma("sync", dbgRows[:, 192:240], gg[:, :, :].rearrange("p a b -> p (a b)"), "ST")
            sc.dma("sync", dbgRows[:, 240:243], dtbb[:, :], "ST")
            sc.dma("sync", dbgRows[:, 243:246], negA[:, :], "ST")
            for ch in range(6):
                sc.dma("sync", dbgQ[ch], q_sb[:, ch, :], "ST")
                sc.dma("sync", dbgKk[ch], k_sb[:, ch, :], "ST")
                sc.dma("sync", dbgV[ch], v_sb[:, ch, :], "ST")
            for ttx in range(16):
                sc.dma("sync", dbgG[ttx], gate[:, ttx, :], "ST")

        for h in range(3):
            sc.op("gpsimd", (lambda h: lambda q: q.memset(S0[h][:], 0.0))(h),
                  wait=[("V", v_phB)], inc="G")
            sc.op("gpsimd", (lambda h: lambda q: q.memset(S1[h][:], 0.0))(h), inc="G")
        Ginit = sc.c("G")
        sv_done = [0, 0, 0]

        for tt in range(n_tiles):
            for hh in range(n_heads):
                Sold = S0[hh] if tt % 2 == 0 else S1[hh]
                Snew = S1[hh] if tt % 2 == 0 else S0[hh]
                kA = k_sb[:, hh * 2 + 0, tt * 128:(tt + 1) * 128]
                kB = k_sb[:, hh * 2 + 1, tt * 128:(tt + 1) * 128]
                qA = q_sb[:, hh * 2 + 0, tt * 128:(tt + 1) * 128]
                qB = q_sb[:, hh * 2 + 1, tt * 128:(tt + 1) * 128]
                vA = v_sb[:, hh * 2 + 0, tt * 128:(tt + 1) * 128]
                vB = v_sb[:, hh * 2 + 1, tt * 128:(tt + 1) * 128]
                ccur = cumg[:, tt, hh:hh + 1]
                bcur = beta[:, tt, hh:hh + 1]

                def trc(q, ccur=ccur):
                    return q.transpose(PR1[:, :], ccur, identf[:, :])
                sc.op("tensor", trc, wait=[("G", Ginit), ("V", sc.c("V")), ("S", sc.c("S"))],
                      inc="T")

                def evc(q):
                    q.copy(out=crow[:, :], in_=PR1[:, :])
                    return q.activation(arow[:, :], PR1[:, :], AF.Exp)
                sc.op("scalar", evc, wait=[("T", sc.c("T"))], inc="S")
                s_c = sc.c("S")

                def bcrow(q):
                    return q.matmul(out=PB4[:, 128:256], lhsT=onesr[:, :], rhs=crow[:, :],
                                    start=True, stop=True)
                sc.op("tensor", bcrow, wait=[("S", s_c)], inc="T")
                t_bcrow = sc.c("T")

                def mkD(q, ccur=ccur):
                    q.tensor_scalar(out=dT[:, :], in0=PB4[:, 128:256], scalar1=ccur,
                                    scalar2=0.0, op0=ALU.subtract, op1=ALU.min)
                    return q.tensor_scalar(out=dU[:, :], in0=PB4[:, 128:256], scalar1=-1.0,
                                           scalar2=ccur, op0=ALU.mult, op1=ALU.add)
                sc.op("vector", mkD, wait=[("T", t_bcrow)], inc="V")

                def mkD2(q):
                    return q.tensor_single_scalar(out=dU[:, :], in_=dU[:, :], scalar=0.0,
                                                  op=ALU.min)
                sc.op("vector", mkD2, wait=[("V", sc.c("V"))], inc="V")

                def expD(q):
                    q.activation(dT[:, :], dT[:, :], AF.Exp)
                    return q.activation(dU[:, :], dU[:, :], AF.Exp)
                sc.op("scalar", expD, wait=[("V", sc.c("V"))], inc="S")
                s_D = sc.c("S")

                def mmkk(q, kA=kA, kB=kB, qA=qA, qB=qB):
                    q.matmul(out=PB0[:, 0:128], lhsT=kA, rhs=kA, start=True, stop=False)
                    q.matmul(out=PB0[:, 0:128], lhsT=kB, rhs=kB, start=False, stop=True)
                    q.matmul(out=PB1[:, 0:128], lhsT=kA, rhs=qA, start=True, stop=False)
                    return q.matmul(out=PB1[:, 0:128], lhsT=kB, rhs=qB, start=False, stop=True)
                sc.op("tensor", mmkk, wait=[("S", s_D), ("V", sc.c("V"))], inc="T")

                def evkk(q):
                    q.copy(out=kk[:, :], in_=PB0[:, 0:128])
                    return q.copy(out=kq[:, :], in_=PB1[:, 0:128])
                sc.op("scalar", evkk, wait=[("T", sc.c("T"))], inc="S")
                s_kk = sc.c("S")

                for fnm in range(5):
                    def mkN_i(q, fnm=fnm, bcur=bcur):
                        if fnm == 0:
                            return q.tensor_scalar(out=kk[:, :], in0=kk[:, :], scalar1=bcur,
                                                   scalar2=None, op0=ALU.mult)
                        if fnm == 1:
                            return q.tensor_tensor(out=kk[:, :], in0=kk[:, :], in1=dU[:, :],
                                                   op=ALU.mult)
                        if fnm == 2:
                            return q.tensor_tensor(out=Nn[:, :], in0=kk[:, :], in1=mLs[:, :],
                                                   op=ALU.mult)
                        if fnm == 3:
                            return q.tensor_tensor(out=kq[:, :], in0=kq[:, :], in1=dT[:, :],
                                                   op=ALU.mult)
                        return q.tensor_tensor(out=pp_sb[:, :], in0=kq[:, :], in1=mUi[:, :],
                                               op=ALU.mult)
                    sc.op("vector", mkN_i, wait=[("S", s_kk), ("V", sc.c("V"))], inc="V")
                v_N = sc.c("V")

                def trN(q):
                    return q.transpose(PTb[:, 0:128], Nn[:, :], idenb[:, :])
                sc.op("tensor", trN, wait=[("V", v_N)], inc="T")

                def evN(q):
                    return q.copy(out=Np[:, :], in_=PTb[:, 0:128])
                sc.op("scalar", evN, wait=[("T", sc.c("T"))], inc="S")
                s_Np = sc.c("S")

                def initT(q):
                    q.tensor_copy(out=Xc[:, :], in_=Nn[:, :])
                    q.tensor_copy(out=Xp[:, :], in_=Np[:, :])
                    return q.tensor_tensor(out=Tp[:, :], in0=Np[:, :], in1=idenb[:, :], op=ALU.add)
                sc.op("vector", initT, wait=[("S", s_Np)], inc="V")
                v_T = sc.c("V")

                for lev in range(6):
                    def dblA(q):
                        q.matmul(out=PB2[:, 0:128], lhsT=Xp[:, :], rhs=Xc[:, :], start=True, stop=True)
                        return q.matmul(out=PB2[:, 128:256], lhsT=Xc[:, :], rhs=Xp[:, :],
                                        start=True, stop=True)
                    sc.op("tensor", dblA, wait=[("V", sc.c("V")), ("S", sc.c("S"))], inc="T")
                    t_dbl = sc.c("T")

                    def evdbl(q):
                        q.copy(out=Xc[:, :], in_=PB2[:, 0:128])
                        return q.copy(out=Xp[:, :], in_=PB2[:, 128:256])
                    sc.op("scalar", evdbl, wait=[("T", t_dbl)], inc="S")
                    s_dbl = sc.c("S")

                    def dblB(q):
                        return q.matmul(out=PB2[:, 256:384], lhsT=Xc[:, :], rhs=Tp[:, :],
                                        start=True, stop=True)
                    sc.op("tensor", dblB, wait=[("S", s_dbl)], inc="T")
                    t_dblB = sc.c("T")

                    def addT(q):
                        return q.tensor_tensor(out=Tp[:, :], in0=Tp[:, :], in1=PB2[:, 256:384],
                                               op=ALU.add)
                    sc.op("vector", addT, wait=[("T", t_dblB)], inc="V")

                def bca(q):
                    return q.matmul(out=PB4[:, 256:384], lhsT=onesr[:, :], rhs=arow[:, :],
                                    start=True, stop=True)
                sc.op("tensor", bca, wait=[("V", sc.c("V")), ("S", sc.c("S"))], inc="T")
                t_bca = sc.c("T")

                def mkkd(q, kA=kA, kB=kB, qA=qA, qB=qB):
                    q.tensor_tensor(out=kd[:, 0, :], in0=kA, in1=PB4[:, 256:384], op=ALU.mult)
                    q.tensor_tensor(out=kd[:, 1, :], in0=kB, in1=PB4[:, 256:384], op=ALU.mult)
                    q.tensor_tensor(out=qd[:, 0, :], in0=qA, in1=PB4[:, 256:384], op=ALU.mult)
                    return q.tensor_tensor(out=qd[:, 1, :], in0=qB, in1=PB4[:, 256:384], op=ALU.mult)
                sc.op("vector", mkkd, wait=[("T", t_bca)], inc="V")
                v_kd = sc.c("V")

                def trkv(q, kA=kA, kB=kB, vA=vA, vB=vB):
                    q.transpose(PTb[:, 0:128], kA, idenb[:, :])
                    q.transpose(PTb[:, 128:256], kB, idenb[:, :])
                    q.transpose(PTb[:, 256:384], vA, idenb[:, :])
                    return q.transpose(PTb[:, 384:512], vB, idenb[:, :])
                sc.op("tensor", trkv, wait=[("V", v_kd), ("S", sc.c("S"))], inc="T")
                t_trkv = sc.c("T")

                def evkv2(q, hh=hh, Sold=Sold):
                    q.copy(out=ktm[:, :], in_=PTb[:, 0:256])
                    q.copy(out=vtm[:, :], in_=PTb[:, 256:512])
                    return q.copy(out=Sbf[:, :, :].rearrange("p a b -> p (a b)"),
                                  in_=(Sold[:, :, :]).rearrange("p a b -> p (a b)"))
                sc.op("scalar", evkv2, wait=[("T", t_trkv), ("V", sv_done[hh]),
                                             ("G", Ginit)], inc="S")
                s_kv2 = sc.c("S")

                def mkkp2(q, tt=tt, hh=hh):
                    return q.tensor_scalar(out=kptm[:, :], in0=ktm[:, :],
                                           scalar1=ecolb[:, tt, hh:hh + 1],
                                           scalar2=None, op0=ALU.mult)
                sc.op("vector", mkkp2, wait=[("S", s_kv2)], inc="V")
                v_kp = sc.c("V")

                def mmkds(q):
                    q.matmul(out=PB3[:, 0:256], lhsT=kd[:, 0, :], rhs=Sbf[:, 0, :],
                             start=True, stop=False)
                    return q.matmul(out=PB3[:, 0:256], lhsT=kd[:, 1, :], rhs=Sbf[:, 1, :],
                                    start=False, stop=True)
                sc.op("tensor", mmkds, wait=[("V", v_kp), ("S", sc.c("S"))], inc="T")
                t_kds = sc.c("T")

                def mkR_a(q):
                    return q.tensor_tensor(out=osc[:, :], in0=vtm[:, :], in1=PB3[:, 0:256],
                                           op=ALU.subtract)
                sc.op("vector", mkR_a, wait=[("T", t_kds)], inc="V")

                def mkR_b(q, bcur=bcur):
                    return q.tensor_scalar(out=Rt[:, :], in0=osc[:, :], scalar1=bcur,
                                           scalar2=None, op0=ALU.mult)
                sc.op("vector", mkR_b, wait=[("V", sc.c("V"))], inc="V")
                v_R = sc.c("V")

                def mmU(q):
                    return q.matmul(out=PB5[:, 0:256], lhsT=Tp[:, :], rhs=Rt[:, :],
                                    start=True, stop=True)
                sc.op("tensor", mmU, wait=[("V", v_R)], inc="T")

                def evU(q):
                    return q.copy(out=Ub[:, :], in_=PB5[:, 0:256])
                sc.op("scalar", evU, wait=[("T", sc.c("T"))], inc="S")
                s_U = sc.c("S")

                def mmO(q):
                    q.matmul(out=PB3[:, 256:512], lhsT=pp_sb[:, :], rhs=Ub[:, :],
                             start=True, stop=False)
                    q.matmul(out=PB3[:, 256:512], lhsT=qd[:, 0, :], rhs=Sbf[:, 0, :],
                             start=False, stop=False)
                    return q.matmul(out=PB3[:, 256:512], lhsT=qd[:, 1, :], rhs=Sbf[:, 1, :],
                                    start=False, stop=True)
                sc.op("tensor", mmO, wait=[("S", s_U)], inc="T")
                t_O = sc.c("T")

                def mmS(q):
                    q.matmul(out=PB0[:, 0:256], lhsT=kptm[:, 0:128], rhs=Ub[:, :],
                             start=True, stop=True)
                    return q.matmul(out=PB1[:, 0:256], lhsT=kptm[:, 128:256], rhs=Ub[:, :],
                                    start=True, stop=True)
                sc.op("tensor", mmS, wait=[("S", s_U), ("V", sc.c("V"))], inc="T")
                t_S = sc.c("T")

                if dbg:
                    dbsc = c_es.enter_context(nc.sbuf_tensor(f"dbsc{tt}_{hh}", [128, 1536], F32)) \
                        if not hasattr(nc, "_dbsc") else nc._dbsc
                    nc._dbsc = dbsc

                    def snap(q, tt=tt, hh=hh, dbsc=dbsc):
                        q.tensor_copy(out=dbsc[:, 0:256], in_=PB0[:, 0:256])
                        q.tensor_copy(out=dbsc[:, 256:512], in_=PB1[:, 0:256])
                        q.tensor_copy(out=dbsc[:, 512:768], in_=Ub[:, :])
                        q.tensor_copy(out=dbsc[:, 768:1024], in_=kptm[:, :])
                        q.tensor_copy(out=dbsc[:, 1024:1280], in_=ktm[:, :])
                        return q.tensor_copy(out=dbsc[:, 1280:1281], in_=onr[:, 0:1])
                    sc.op("vector", snap, wait=[("T", t_S)], inc="V")
                    v_sn = sc.c("V")
                    sc.dma("sync", dbgP[tt * 3 + hh], dbsc[:, 0:512], "ST", wait=[("V", v_sn)])
                    sc.dma("sync", dbgU[tt * 3 + hh], dbsc[:, 512:768], "ST", wait=[("V", v_sn)])
                    sc.dma("sync", dbgK[tt * 3 + hh], dbsc[:, 768:1024], "ST", wait=[("V", v_sn)])
                    sc.dma("sync", dbgK2[tt * 3 + hh], dbsc[:, 1024:1411], "ST", wait=[("V", v_sn)])

                def evS(q, Sold=Sold, Snew=Snew, tt=tt, hh=hh):
                    q.scalar_tensor_tensor(out=Snew[:, 0, :], in0=Sold[:, 0, :],
                                           scalar=acolb[:, tt, hh:hh + 1], in1=PB0[:, 0:256],
                                           op0=ALU.mult, op1=ALU.add)
                    return q.scalar_tensor_tensor(out=Snew[:, 1, :], in0=Sold[:, 1, :],
                                                  scalar=acolb[:, tt, hh:hh + 1], in1=PB1[:, 0:256],
                                                  op0=ALU.mult, op1=ALU.add)
                sc.op("vector", evS, wait=[("T", t_S)], inc="V")
                sv_done[hh] = sc.c("V")
                if dbg:
                    sc.dma("sync", dbgS[tt * 3 + hh],
                           Snew[:, :, :].rearrange("p a b -> p (a b)"), "ST",
                           wait=[("V", sv_done[hh])])

                def sqo(q):
                    return q.activation(osc[:, :], PB3[:, 256:512], AF.Square,
                                        accum_out=onr[:, 0:1])
                sc.op("scalar", sqo, wait=[("T", t_O)], inc="S")

                def rstd_o(q):
                    q.activation(onr[:, 1:2], onr[:, 0:1], AF.Ln, scale=1.0 / 256.0,
                                 bias=epsc[:, 0:1])
                    return q.activation(onr[:, 1:2], onr[:, 1:2], AF.Exp, scale=-0.5)
                sc.op("scalar", rstd_o, wait=[("S", sc.c("S"))], inc="S")
                s_on = sc.c("S")

                def evo_a(q):
                    return q.tensor_scalar(out=osc[:, :], in0=PB3[:, 256:512], scalar1=onr[:, 1:2],
                                           scalar2=None, op0=ALU.mult)
                sc.op("vector", evo_a, wait=[("S", s_on)], inc="V")

                def evo_b(q):
                    return q.tensor_tensor(out=osc[:, :], in0=osc[:, :], in1=wnb[:, :], op=ALU.mult)
                sc.op("vector", evo_b, wait=[("V", sc.c("V"))], inc="V")

                def evo_c(q, tt=tt, hh=hh):
                    return q.tensor_tensor(out=oout[:, :], in0=osc[:, :],
                                           in1=gate[:, tt, hh * 256:(hh + 1) * 256], op=ALU.mult)
                sc.op("vector", evo_c, wait=[("V", sc.c("V"))], inc="V")
                v_o = sc.c("V")

                sc.dma("sync", o_out[tt][:, hh * 256:(hh + 1) * 256], oout[:, :], "ST",
                       wait=[("V", v_o)])
                sc.op("vector", lambda q: q.engine_nop(), wait=[("ST", sc.c("ST"))], inc="V")

        sc.op("vector", lambda q: q.engine_nop(), wait=[("ST", sc.c("ST"))])
        block = es.enter_context(nc.Block())
        sc.emit_all(block, sems)
        block = None
        pc_es.close()
        c_es.close()
        qv_es.close()
    return nc


def gdn_host_inputs(inputs, x2T):
    wq = np.ascontiguousarray(np.asarray(inputs["L1_attn_Wq"], dtype=np.float32).T)
    wk = np.ascontiguousarray(np.asarray(inputs["L1_attn_Wk"], dtype=np.float32).T)
    wv = np.ascontiguousarray(np.asarray(inputs["L1_attn_Wv"], dtype=np.float32).T)
    wg = np.ascontiguousarray(np.asarray(inputs["L1_attn_Wg"], dtype=np.float32).T)
    wb = np.asarray(inputs["L1_attn_Wb"], dtype=np.float32).T
    wa = np.asarray(inputs["L1_attn_Wa"], dtype=np.float32).T
    cq = np.asarray(inputs["L1_attn_conv_q"], dtype=np.float32)
    ck = np.asarray(inputs["L1_attn_conv_k"], dtype=np.float32)
    cv = np.asarray(inputs["L1_attn_conv_v"], dtype=np.float32)
    onw = np.asarray(inputs["L1_attn_o_norm_w"], dtype=np.float32)[None, :]
    alog = np.asarray(inputs["L1_attn_A_log"], dtype=np.float32)
    dtb = np.asarray(inputs["L1_attn_dt_bias"], dtype=np.float32)
    iden = np.eye(128, dtype=np.float32)
    sl = np.tril(np.ones((128, 128), np.float32), -1)
    mskLs = np.ascontiguousarray(-sl)
    mskUi = np.ascontiguousarray(np.triu(np.ones((128, 128), np.float32), 0))
    triu1 = np.ascontiguousarray(np.triu(np.ones((128, 128), np.float32), 0))
    maps = []
    for c in range(8):
        b, hg = c // 2, c % 2
        hsl = slice(hg * 768, (hg + 1) * 768)
        hsl3 = slice(hg * 3, (hg + 1) * 3)
        maps.append(dict(
            xseq=np.ascontiguousarray(x2T[:, b * 2048:(b + 1) * 2048]).reshape(16, 128, 2048),
            wqT=np.ascontiguousarray(wq[:, hsl]), wkT=np.ascontiguousarray(wk[:, hsl]),
            wvT=np.ascontiguousarray(wv[:, hsl]), wgT=np.ascontiguousarray(wg[:, hsl]),
            wbaT=np.ascontiguousarray(np.concatenate([wb[:, hsl3], wa[:, hsl3]], axis=1)),
            convq=np.ascontiguousarray(cq[hsl]), convk=np.ascontiguousarray(ck[hsl]),
            convv=np.ascontiguousarray(cv[hsl]),
            alog=np.ascontiguousarray(alog[hsl3])[None, :],
            dtb=np.ascontiguousarray(dtb[hsl3])[None, :],
            onw=onw, iden=iden, mskLs=mskLs, mskUi=mskUi, triu1=triu1))
    return maps


def run_gdn(inputs, x2T):
    nc = build_gdn()
    maps = gdn_host_inputs(inputs, x2T)
    res = run_bass_kernel_spmd(nc, maps, core_ids=list(range(8)))
    o = np.zeros((8192, 1536), dtype=bfloat16)
    for c in range(8):
        b, hg = c // 2, c % 2
        o[b * 2048:(b + 1) * 2048, hg * 768:(hg + 1) * 768] = \
            np.asarray(res.results[c]["o_out"]).reshape(2048, 768)
    return o
